# version 13
# baseline (speedup 1.0000x reference)
import numpy as np
import ml_dtypes
import concourse.bass as bass
import concourse.mybir as mybir
from concourse.bass_utils import run_bass_kernel_spmd

F32 = mybir.dt.float32
BF16 = mybir.dt.bfloat16
I32 = mybir.dt.int32
AF = mybir.ActivationFunctionType
ALU = mybir.AluOpType
AX = mybir.AxisListType
BF = ml_dtypes.bfloat16

B, T, D = 8, 4096, 1024
NT, ND = T // 128, D // 128
H, DH = 16, 64
CH = 512
NCH = T // CH

CONST_VALS = [
    -12.0,
    1.0 / 32.0,
    1e-5,
    1e-6,
    -0.5,
    1.0,
    0.0,
    1.0 / 1024.0,
    -1.0,
    0.5,
    1024.0 / 1023.0,
    1e-12,
    8.0,
    -128.0,
]
NCONST = 16
SBUF_LO = 16512
SBUF_HI = 229312

ENGS = ("sync", "scalar", "tensor", "vector", "gpsimd")


class Sem:
    def __init__(self, nc, name):
        self.h = nc.alloc_semaphore(name)
        self.n = 0


def _ap_name(a):
    try:
        return a.tensor.name
    except AttributeError:
        return None


_IO = {
    "tensor_copy": ((0,), (1,)),
    "tensor_tensor": ((0,), (1, 2)),
    "tensor_add": ((0,), (1, 2)),
    "tensor_sub": ((0,), (1, 2)),
    "tensor_mul": ((0,), (1, 2)),
    "tensor_max": ((0,), (1, 2)),
    "tensor_scalar": ((0,), (1, 2, 3)),
    "tensor_single_scalar": ((0,), (1, 2)),
    "tensor_scalar_min": ((0,), (1, 2)),
    "tensor_scalar_max": ((0,), (1, 2)),
    "tensor_scalar_add": ((0,), (1, 2)),
    "tensor_scalar_mul": ((0,), (1, 2)),
    "scalar_tensor_tensor": ((0,), (1, 2, 3)),
    "tensor_tensor_scan": ((0,), (1, 2, 3)),
    "tensor_reduce": ((0,), (1,)),
    "reduce_sum": ((0,), (1,)),
    "reduce_max": ((0,), (1,)),
    "memset": ((0,), ()),
    "memzero": ((0,), ()),
    "reciprocal": ((0,), (1,)),
    "select": ((0,), (1, 2, 3)),
    "copy_predicated": ((0,), (1, 2)),
    "stream_shuffle": ((0,), (1,)),
    "iota": ((0,), ()),
    "transpose": ((0,), (1, 2)),
    "matmul": ((0,), (1, 2)),
    "activation": (("out", "accum_out"), ("in_", "bias", "scale")),
    "dma_start": (("out",), ("in_",)),
}

ALL_CHAINS = ("scalar", "vector", "tensor", "gpsimd")


class Prog:

    def __init__(self, nc):
        self.nc = nc
        self.chain = {e: Sem(nc, "c_" + e) for e in ALL_CHAINS}
        self.pool = {}
        for q, n in (("sync", 10), ("scalar", 6), ("gpsimd", 6)):
            self.pool[q] = [Sem(nc, f"dma_{q}{i}") for i in range(n)]
        self._rr = {q: 0 for q in self.pool}
        self.ops = []
        self.per_eng_ops = {e: [] for e in ENGS}
        self.last_w = {}
        self.rsince = {}
        self.live = {}
        self.dead = {}
        self.alias_init = {}
        self._fence = None
        self._eng_last = {}
        self._cost = 500
        self._bytes = 0
        self.lo = SBUF_LO
        self.hi = SBUF_HI
        self._nm = 0
        self.pb = [nc.alloc_psum_tensor(f"pb{i}", [128, 512], F32) for i in range(8)]
        self.pbh = [t[:].bitcast(BF16) for t in self.pb]
        self.dbg = {}

    def sb(self, shape, dt=F32, side="left", name=None):
        nbytes = int(np.prod(shape[1:])) * mybir.dt.size(dt)
        nbytes = (nbytes + 31) & ~31
        self._nm += 1
        if side == "left":
            off = self.lo
            self.lo += nbytes
        else:
            self.hi -= nbytes
            off = self.hi
        assert self.lo <= self.hi, f"SBUF overflow: lo={self.lo} hi={self.hi}"
        nm = name or f"sb{self._nm}"
        span = (off, off + nbytes)
        h = self.nc.alloc_sbuf_tensor_at(nm, list(shape), dt, offset=off)
        real = h.name
        init = set()
        for dn, (dlo, dhi, deps) in list(self.dead.items()):
            if dlo < span[1] and span[0] < dhi:
                init.update(deps)
                if dlo >= span[0] and dhi <= span[1]:
                    del self.dead[dn]
        if init:
            self.alias_init[real] = init
        self.live[real] = span
        return h

    def mark(self, side="left"):
        return self.lo if side == "left" else self.hi

    def release(self, mark, side="left"):
        if side == "left":
            span = (mark, self.lo)
            self.lo = mark
        else:
            span = (self.hi, mark)
            self.hi = mark
        for nm, (tlo, thi, *_r) in list(self.live.items()):
            if tlo >= span[0] and thi <= span[1]:
                deps = set(self.rsince.get(nm, ()))
                w = self.last_w.get(nm)
                if w is not None:
                    deps.add(w)
                prior = self.alias_init.pop(nm, None)
                if prior:
                    deps.update(prior)
                if len(deps) > 48:
                    deps = {self._join(deps)}
                self.dead[nm + f"@{len(self.dead)}"] = (tlo, thi, deps)
                del self.live[nm]
                self.last_w.pop(nm, None)
                self.rsince.pop(nm, None)

    def _read(self, deps, name):
        w = self.last_w.get(name)
        if w is not None:
            deps.add(w)
        ai = self.alias_init.get(name)
        if ai:
            deps.update(ai)

    def _note_read(self, name, opid):
        self.rsince.setdefault(name, []).append(opid)

    def _write(self, deps, name):
        w = self.last_w.get(name)
        if w is not None:
            deps.add(w)
        deps.update(self.rsince.get(name, ()))
        ai = self.alias_init.get(name)
        if ai:
            deps.update(ai)

    def _note_write(self, name, opid):
        self.last_w[name] = opid
        self.rsince[name] = []
        self.alias_init.pop(name, None)

    def _join(self, deps):
        opid = len(self.ops)
        rec = {"id": opid, "eng": "gpsimd",
               "fn": lambda e: e.engine_nop(),
               "deps": set(deps), "odeps": set(), "key": ("eng", "gpsimd"),
               "inc": 1, "cost": 60, "bytes": 0}
        self.ops.append(rec)
        self._eng_last["gpsimd"] = opid
        return opid

    def _emit(self, eng, fn, reads, writes, prod_key):
        opid = len(self.ops)
        deps = set()
        for nm in reads:
            self._read(deps, nm)
        for nm in writes:
            self._write(deps, nm)
        odeps = set()
        if eng == "tensor":
            te = {d for d in deps if self.ops[d]["eng"] == "tensor"}
            deps -= te
            odeps |= te
        if self._fence is not None:
            odeps.add(self._fence)
        rec = {"id": opid, "eng": eng, "fn": fn, "deps": deps, "odeps": odeps,
               "key": prod_key, "inc": 16 if prod_key[0] == "dma" else 1,
               "cost": self._cost, "bytes": self._bytes}
        self.ops.append(rec)
        self._eng_last[eng] = opid
        for nm in reads:
            self._note_read(nm, opid)
        for nm in writes:
            self._note_write(nm, opid)
        return opid

    def op(self, eng, meth, *args, **kw):
        wspec, rspec = _IO[meth]
        reads, writes = [], []
        def grab(spec, out):
            for s in spec:
                a = kw.get(s) if isinstance(s, str) else (args[s] if s < len(args) else None)
                nm = _ap_name(a) if a is not None else None
                if nm is not None:
                    out.append(nm)
        grab(wspec, writes)
        grab(rspec, reads)
        try:
            a0 = kw.get("out") if "out" in kw else args[0]
            F = int(a0.free_size()) if hasattr(a0, "free_size") else 512
        except Exception:
            F = 512
        if eng == "tensor":
            c = max(90, int(F * 0.45) + 50)
        elif eng == "scalar":
            c = int(190 + F / 1.2)
        elif eng == "gpsimd":
            c = int(500 + F / 1.2)
        else:
            c = int(160 + F / 0.96)
        self._cost = c
        self._bytes = 0

        def fn(e, meth=meth, args=args, kw=kw):
            return getattr(e, meth)(*args, **kw)
        return self._emit(eng, fn, reads, writes, ("eng", eng))

    def dma(self, q, out, in_):
        pool = self.pool[q]
        sem = pool[self._rr[q]]
        self._rr[q] = (self._rr[q] + 1) % len(pool)
        slot = "slot:" + sem.h.name if hasattr(sem.h, "name") else "slot:" + str(id(sem))

        def fn(e, out=out, in_=in_):
            return e.dma_start(out=out, in_=in_)
        nm_o, nm_i = _ap_name(out), _ap_name(in_)
        try:
            self._bytes = int(out.nbytes() if hasattr(out, "nbytes") else 0)
        except Exception:
            self._bytes = 65536
        self._cost = 300
        return self._emit(q, fn, [nm_i, slot], [nm_o, slot], ("dma", id(sem), sem))

    def barrier(self):
        opid = len(self.ops)
        rec = {"id": opid, "eng": "BARRIER", "fn": None, "deps": set(),
               "odeps": set(self._eng_last.values()), "key": None, "inc": 0,
               "cost": 0, "bytes": 0}
        self.ops.append(rec)
        self.last_w.clear()
        self.rsince.clear()
        self._fence = opid

    def finalize(self):
        import heapq
        ops = self.ops
        n = len(ops)
        preds = [None] * n
        succs = [[] for _ in range(n)]
        indeg = [0] * n
        real = []
        for o in ops:
            i = o["id"]
            if o["eng"] == "BARRIER":
                preds[i] = set()
                continue
            ps = set(o["deps"]) | set(o["odeps"])
            ps = {p for p in ps if ops[p]["eng"] != "BARRIER"}
            preds[i] = ps
            for p_ in ps:
                succs[p_].append(i)
            indeg[i] = len(ps)
            real.append(i)
        prio = [0.0] * n
        for i in reversed(range(n)):
            if ops[i]["eng"] == "BARRIER":
                continue
            m = 0.0
            for s_ in succs[i]:
                if prio[s_] > m:
                    m = prio[s_]
            prio[i] = ops[i]["cost"] + m
        SEM_LAT = 150.0
        heaps = {e: [] for e in ENGS}
        t_eng = {e: 0.0 for e in ENGS}
        comp = [0.0] * n
        for i in real:
            if indeg[i] == 0:
                heapq.heappush(heaps[ops[i]["eng"]], (-prio[i], i))
        sched = {e: [] for e in ENGS}
        remaining = len(real)
        while remaining:
            best_e, best_start = None, None
            for e in ENGS:
                if not heaps[e]:
                    continue
                i = heaps[e][0][1]
                st = t_eng[e]
                for p_ in preds[i]:
                    if comp[p_] > st:
                        st = comp[p_]
                if best_start is None or st < best_start:
                    best_e, best_start = e, st
            e = best_e
            _, i = heapq.heappop(heaps[e])
            st = t_eng[e]
            for p_ in preds[i]:
                if comp[p_] > st:
                    st = comp[p_]
            o = ops[i]
            fin = st + o["cost"]
            t_eng[e] = fin
            lat = SEM_LAT
            if o["key"][0] == "dma":
                lat = 1300.0 + o["bytes"] / 358.0
            comp[i] = fin + lat
            sched[e].append(i)
            remaining -= 1
            for s_ in succs[i]:
                indeg[s_] -= 1
                if indeg[s_] == 0:
                    heapq.heappush(heaps[ops[s_]["eng"]], (-prio[s_], s_))
        self.makespan_ns = max(t_eng.values())
        need = [False] * n
        for o in ops:
            if o["eng"] == "BARRIER":
                continue
            for pid in o["deps"]:
                need[pid] = True
        for e in ENGS:
            if sched[e]:
                need[sched[e][-1]] = True
        val = [0] * n
        counts = {}
        for e in ENGS:
            for i in sched[e]:
                o = ops[i]
                if not need[i]:
                    continue
                k = o["key"]
                counts[k] = counts.get(k, 0) + o["inc"]
                val[i] = counts[k]

        def sem_of_key(k):
            return k[2].h if k[0] == "dma" else self.chain[k[1]].h

        streams = {e: [] for e in ENGS}
        for e in ENGS:
            seen = {}
            for i in sched[e]:
                o = ops[i]
                byk = {}
                for pid in o["deps"]:
                    assert need[pid], f"dep on un-inc'd op {pid}"
                    kk = ops[pid]["key"]
                    v = val[pid]
                    if byk.get(kk, 0) < v:
                        byk[kk] = v
                for kk, v in sorted(byk.items(), key=lambda kv: kv[1]):
                    if seen.get(kk, 0) >= v:
                        continue
                    h = sem_of_key(kk)
                    streams[e].append(lambda eng_, h=h, v=v: eng_.wait_ge(h, v))
                    seen[kk] = v
                if need[i]:
                    h = sem_of_key(o["key"])
                    inc = o["inc"]
                    fn = o["fn"]
                    def fi(eng_, fn=fn, h=h, inc=inc):
                        fn(eng_).then_inc(h, inc)
                    streams[e].append(fi)
                else:
                    streams[e].append(o["fn"])
            todo = []
            for k, v in counts.items():
                if seen.get(k, 0) < v:
                    todo.append((sem_of_key(k), v))
            def fb(eng_, todo=todo):
                for h, v in todo:
                    eng_.wait_ge(h, v)
            streams[e].append(fb)
        with self.nc.Block() as blk:
            for e in ENGS:
                def body(eng_, fns=streams[e]):
                    for f in fns:
                        f(eng_)
                getattr(blk, e)(body)


def host_consts():
    cst = np.zeros((128, NCONST), np.float32)
    for i, v in enumerate(CONST_VALS):
        cst[:, i] = v
    return {
        "consts": cst,
        "identb": np.eye(128, dtype=BF),
        "identf": np.eye(128, dtype=np.float32),
        "ones_row_b": np.ones((1, 128), BF),
        "ones_row_f": np.ones((1, 128), np.float32),
        "ones_col_b": np.ones((128, 1), BF),
    }


def declare_const_inputs(nc):
    d = {}
    d["consts"] = nc.dram_tensor("consts", [128, NCONST], F32, kind="ExternalInput")
    d["identb"] = nc.dram_tensor("identb", [128, 128], BF16, kind="ExternalInput")
    d["identf"] = nc.dram_tensor("identf", [128, 128], F32, kind="ExternalInput")
    d["ones_row_b"] = nc.dram_tensor("ones_row_b", [1, 128], BF16, kind="ExternalInput")
    d["ones_row_f"] = nc.dram_tensor("ones_row_f", [1, 128], F32, kind="ExternalInput")
    d["ones_col_b"] = nc.dram_tensor("ones_col_b", [128, 1], BF16, kind="ExternalInput")
    return d


class CommonSB:

    def __init__(self, p: Prog, dr):
        self.cst = p.sb([128, NCONST], F32)
        self.identb = p.sb([128, 128], BF16)
        self.identf = p.sb([128, 128], F32)
        self.ones_row_b = p.sb([1, 128], BF16)
        self.ones_row_f = p.sb([1, 128], F32)
        self.ones_col_b = p.sb([128, 1], BF16)
        self.ready = []
        for t, k in [(self.cst, "consts"), (self.identb, "identb"), (self.identf, "identf"),
                     (self.ones_row_b, "ones_row_b"), (self.ones_row_f, "ones_row_f"),
                     (self.ones_col_b, "ones_col_b")]:
            self.ready.append(p.dma("sync", t[:], dr[k][:]))

    def c(self, i):
        return self.cst[:, i:i + 1]

    def c1(self, i):
        return self.cst[0:1, i:i + 1]



def tok_ln_pair(p, cm, xb, sq, xn0, stat, eps1_col, eps2_col):
    p.op("vector", "reduce_sum", stat[:, 0:1], xb[:], axis=AX.X)
    p.op("vector", "tensor_scalar", stat[:, 1:2], stat[:, 0:1], 1.0 / 1024.0, None, ALU.mult)
    p.op("scalar", "activation", out=sq[:], in_=xb[:], func=AF.Square)
    p.op("vector", "reduce_sum", stat[:, 2:3], sq[:], axis=AX.X)
    p.op("vector", "tensor_mul", stat[:, 3:4], stat[:, 1:2], stat[:, 1:2])
    p.op("vector", "scalar_tensor_tensor", stat[:, 4:5], stat[:, 2:3], 1.0 / 1024.0,
         stat[:, 3:4], ALU.mult, ALU.subtract)
    p.op("scalar", "activation", out=stat[:, 5:6], in_=stat[:, 4:5], func=AF.Ln, bias=eps1_col)
    p.op("scalar", "activation", out=stat[:, 5:6], in_=stat[:, 5:6], func=AF.Exp, scale=cm.c(4))
    p.op("vector", "tensor_scalar", xn0[:], xb[:], stat[:, 1:2], stat[:, 5:6],
         ALU.subtract, ALU.mult)
    p.op("vector", "reduce_sum", stat[:, 6:7], xn0[:], axis=AX.X)
    p.op("vector", "tensor_scalar", stat[:, 7:8], stat[:, 6:7], 1.0 / 1024.0, None, ALU.mult)
    p.op("scalar", "activation", out=sq[:], in_=xn0[:], func=AF.Square)
    p.op("vector", "reduce_sum", stat[:, 8:9], sq[:], axis=AX.X)
    p.op("vector", "tensor_mul", stat[:, 9:10], stat[:, 7:8], stat[:, 7:8])
    p.op("vector", "scalar_tensor_tensor", stat[:, 10:11], stat[:, 8:9], 1.0 / 1024.0,
         stat[:, 9:10], ALU.mult, ALU.subtract)
    p.op("scalar", "activation", out=stat[:, 11:12], in_=stat[:, 10:11], func=AF.Ln, bias=eps2_col)
    p.op("scalar", "activation", out=stat[:, 11:12], in_=stat[:, 11:12], func=AF.Exp, scale=cm.c(4))


def stage_a(p: Prog, cm: CommonSB, x_dr, ls_dr, lb_dr, wq_dr, wk_dr, wv_dr,
            x1T_dr, scr_dr):
    mark0 = p.mark()
    xinT = [p.sb([128, T], BF16, name=f"xinT{j}") for j in range(ND)]
    p.dbg["xinT"] = xinT

    m_setup = p.mark()
    lsr = p.sb([1, D], F32)
    lbr = p.sb([1, D], F32)
    srep = p.sb([128, D], F32)
    brep = p.sb([128, D], F32)
    p.dma("sync", lsr[:], ls_dr[:])
    p.dma("sync", lbr[:], lb_dr[:])
    for half in range(2):
        sli = slice(half * 512, half * 512 + 512)
        p.op("tensor", "matmul", p.pb[0][:, :], cm.ones_row_f[:], lsr[:, sli],
             start=True, stop=True)
        p.op("tensor", "matmul", p.pb[1][:, :], cm.ones_row_f[:], lbr[:, sli],
             start=True, stop=True)
        p.op("vector", "tensor_copy", srep[:, sli], p.pb[0][:, :])
        p.op("vector", "tensor_copy", brep[:, sli], p.pb[1][:, :])

    xsb = [p.sb([128, D], F32) for _ in range(2)]
    sq = p.sb([128, D], F32)
    xn0 = p.sb([128, D], F32)
    tmp = p.sb([128, D], F32)
    xin_b = [p.sb([128, D], BF16) for _ in range(2)]
    stat = p.sb([128, 16], F32)

    for i in range(NT):
        xb = xsb[i % 2]
        p.dma("sync", xb[:], x_dr[i * 128:(i + 1) * 128, :])
        tok_ln_pair(p, cm, xb, sq, xn0, stat, cm.c(2), cm.c(3))
        p.op("vector", "tensor_scalar", tmp[:], xn0[:], stat[:, 7:8], stat[:, 11:12],
             ALU.subtract, ALU.mult)
        p.op("vector", "tensor_mul", tmp[:], tmp[:], srep[:])
        p.op("vector", "tensor_add", xin_b[i % 2][:], tmp[:], brep[:])
        for j in range(ND):
            p.op("tensor", "transpose", p.pbh[j % 2][:, 0:128],
                 xin_b[i % 2][:, j * 128:(j + 1) * 128], cm.identb[:])
            p.op("scalar", "activation", out=xinT[j][:, i * 128:(i + 1) * 128],
                 in_=p.pbh[j % 2][:, 0:128], func=AF.Copy)
    p.release(m_setup)

    m_half = p.mark()
    wst = [p.sb([128, 512], F32) for _ in range(2)]
    wkvb = [p.sb([128, 512], BF16) for _ in range(ND)]
    wqb = [p.sb([128, 512], BF16) for _ in range(ND)]
    Vh = [p.sb([128, 8 * 65], BF16) for _ in range(NT)]
    kTh = [p.sb([128, T], BF16) for _ in range(4)]
    qTh = [p.sb([128, CH], BF16) for _ in range(4)]
    qp = [p.sb([128, CH], BF16) for _ in range(2)]
    probs = [p.sb([128, CH], BF16) for _ in range(4)]
    sum_f = [p.sb([128, CH], F32) for _ in range(4)]
    cast_b = p.sb([128, CH], BF16)
    den_rec = p.sb([1, CH], F32)
    den_h = p.sb([1, CH], BF16)
    den_l = p.sb([1, CH], BF16)
    rrep = p.sb([64, CH], F32)
    rowA = p.sb([1, CH], F32)
    rowB = p.sb([1, CH], F32)
    rowC = den_rec
    sumh0_sb = [p.sb([128, CH], F32) for _ in range(4)]
    xsb2 = [p.sb([128, D], F32) for _ in range(1)]
    xTc = [p.sb([128, CH], F32) for _ in range(ND)]

    def load_weight_half(w_dr, half, dest):
        for din in range(ND):
            p.dma("scalar", wst[din % 2][:],
                  w_dr[din * 128:(din + 1) * 128, half * 512:half * 512 + 512])
            p.op("gpsimd", "tensor_copy", dest[din][:], wst[din % 2][:])

    for half in range(2):
        jt0 = 4 * half

        load_weight_half(wv_dr, half, wkvb)
        for tt in range(NT):
            bank = 2 + tt % 2
            for din in range(ND):
                p.op("tensor", "matmul", p.pb[bank][:, :],
                     xinT[din][:, tt * 128:(tt + 1) * 128], wkvb[din][:],
                     start=(din == 0), stop=(din == ND - 1))
            vv = Vh[tt][:].rearrange("p (h c) -> p h c", c=65)
            p.op("vector", "tensor_copy", vv[:, :, 0:64], p.pb[bank][:, :])
            p.op("vector", "memset", vv[:, :, 64:65], 1.0)

        load_weight_half(wk_dr, half, wkvb)
        it = 0
        for dt in range(4):
            for cc in range(NCH):
                bank = it % 2
                for din in range(ND):
                    p.op("tensor", "matmul", p.pb[bank][:, :],
                         wkvb[din][:, dt * 128:(dt + 1) * 128],
                         xinT[din][:, cc * CH:(cc + 1) * CH],
                         start=(din == 0), stop=(din == ND - 1))
                p.op("scalar", "activation", out=kTh[dt][:, cc * CH:(cc + 1) * CH],
                     in_=p.pb[bank][:, :], func=AF.Copy)
                it += 1

        load_weight_half(wq_dr, half, wqb)

        for cc in range(NCH):
            for dt in range(4):
                bank = 6 + dt % 2
                for din in range(ND):
                    p.op("tensor", "matmul", p.pb[bank][:, :],
                         wqb[din][:, dt * 128:(dt + 1) * 128],
                         xinT[din][:, cc * CH:(cc + 1) * CH],
                         start=(din == 0), stop=(din == ND - 1))
                p.op("scalar", "activation", out=qTh[dt][:],
                     in_=p.pb[bank][:, :], func=AF.Copy)
            if cc == 0 and half == 0:
                p.op("vector", "memset", qp[0][64:128, :], 0.0)
                p.op("vector", "memset", qp[1][0:64, :], 0.0)
            for hh in range(8):
                ht = hh // 2
                hb = 64 * (hh % 2)
                p.op("vector", "tensor_copy", qp[hh % 2][hb:hb + 64, :],
                     qTh[ht][hb:hb + 64, :])
                def a_qk(kj):
                    bank = kj % 4
                    p.op("tensor", "matmul", p.pb[bank][:, :],
                         kTh[ht][:, kj * 128:(kj + 1) * 128],
                         qp[hh % 2][:],
                         start=True, stop=True)
                    p.op("scalar", "activation", out=probs[kj % 4][:],
                         in_=p.pb[bank][:, :], func=AF.Exp,
                         scale=cm.c(1), bias=cm.c(0))

                def a_pv(kj):
                    vv = Vh[kj][:].rearrange("p (h c) -> p h c", c=65)[:, hh, :]
                    p.op("tensor", "matmul", p.pb[4][0:65, :], vv, probs[kj % 4][:],
                         start=(kj == 0), stop=(kj == NT - 1), skip_group_check=True)
                for kj in range(NT):
                    a_qk(kj)
                    if kj >= 2:
                        a_pv(kj - 2)
                a_pv(NT - 2)
                a_pv(NT - 1)
                p.op("scalar", "activation", out=den_rec[:], in_=p.pb[4][64:65, :], func=AF.Ln)
                p.op("scalar", "activation", out=den_rec[:], in_=den_rec[:], func=AF.Exp,
                     scale=cm.c1(8))
                p.op("scalar", "activation", out=den_h[:], in_=den_rec[:], func=AF.Copy)
                p.op("vector", "tensor_sub", den_l[:], den_rec[:], den_h[:])
                p.op("tensor", "matmul", p.pb[5][0:64, :], cm.ones_row_b[:, 0:64],
                     den_h[:], start=True, stop=False, skip_group_check=True)
                p.op("tensor", "matmul", p.pb[5][0:64, :], cm.ones_row_b[:, 0:64],
                     den_l[:], start=False, stop=True, skip_group_check=True)
                p.op("vector", "tensor_copy", rrep[:], p.pb[5][0:64, :])
                p.op("vector", "scalar_tensor_tensor",
                     sum_f[ht][hb:hb + 64, :], p.pb[4][0:64, :], 1.0, rrep[:],
                     ALU.mult, ALU.mult)
            for dt in range(4):
                j = jt0 + dt
                p.op("vector", "tensor_add", sum_f[dt][:], sum_f[dt][:],
                     xinT[j][:, cc * CH:(cc + 1) * CH])
                p.op("scalar", "activation", out=cast_b[:], in_=sum_f[dt][:], func=AF.Copy)
                p.op("tensor", "matmul", p.pb[0][0:1, :], cm.ones_col_b[:], cast_b[:],
                     start=(dt == 0), stop=(dt == 3), skip_group_check=True)
                p.op("scalar", "activation", out=cast_b[:], in_=sum_f[dt][:], func=AF.Square)
                p.op("tensor", "matmul", p.pb[1][0:1, :], cm.ones_col_b[:], cast_b[:],
                     start=(dt == 0), stop=(dt == 3), skip_group_check=True)
            if half == 0:
                p.op("vector", "tensor_copy", rowA[:], p.pb[0][0:1, :])
                p.op("vector", "tensor_copy", rowB[:], p.pb[1][0:1, :])
                for dt in range(4):
                    p.dma("gpsimd", scr_dr[dt * 128:(dt + 1) * 128, cc * CH:(cc + 1) * CH],
                          sum_f[dt][:])
                p.dma("gpsimd", scr_dr[512:513, cc * CH:(cc + 1) * CH], rowA[:])
                p.dma("gpsimd", scr_dr[513:514, cc * CH:(cc + 1) * CH], rowB[:])
            else:
                p.dma("sync", rowA[:], scr_dr[512:513, cc * CH:(cc + 1) * CH])
                p.dma("sync", rowB[:], scr_dr[513:514, cc * CH:(cc + 1) * CH])
                for dt in range(4):
                    p.dma("sync", sumh0_sb[dt][:],
                          scr_dr[dt * 128:(dt + 1) * 128, cc * CH:(cc + 1) * CH])
                for q4 in range(4):
                    p.dma("sync", xsb2[0][:],
                          x_dr[cc * CH + q4 * 128: cc * CH + (q4 + 1) * 128, :])
                    for j in range(ND):
                        gi = q4 * ND + j
                        bank = 6 + gi % 2
                        p.op("tensor", "transpose", p.pb[bank][:, 0:128],
                             xsb2[0][:, j * 128:(j + 1) * 128], cm.identf[:])
                        p.op("vector", "tensor_copy",
                             xTc[j][:, q4 * 128:(q4 + 1) * 128], p.pb[bank][:, 0:128])
                p.op("vector", "scalar_tensor_tensor", rowA[:], p.pb[0][0:1, :], 1.0,
                     rowA[:], ALU.bypass, ALU.add)
                p.op("vector", "scalar_tensor_tensor", rowB[:], p.pb[1][0:1, :], 1.0,
                     rowB[:], ALU.bypass, ALU.add)
                p.op("vector", "tensor_scalar", rowA[:], rowA[:], 1.0 / 1024.0, None, ALU.mult)
                p.op("vector", "tensor_scalar", rowB[:], rowB[:], 1.0 / 1024.0, None, ALU.mult)
                p.op("vector", "tensor_mul", rowC[:], rowA[:], rowA[:])
                p.op("vector", "tensor_sub", rowB[:], rowB[:], rowC[:])
                p.op("scalar", "activation", out=rowB[:], in_=rowB[:], func=AF.Ln, bias=cm.c1(2))
                p.op("scalar", "activation", out=rowB[:], in_=rowB[:], func=AF.Exp, scale=cm.c1(4))
                p.op("tensor", "matmul", p.pb[2][:, :], cm.ones_row_f[:], rowA[:],
                     start=True, stop=True)
                p.op("tensor", "matmul", p.pb[3][:, :], cm.ones_row_f[:], rowB[:],
                     start=True, stop=True)
                for j in range(ND):
                    src = sumh0_sb[j] if j < 4 else sum_f[j - 4]
                    p.op("vector", "scalar_tensor_tensor", src[:], src[:], 1.0,
                         p.pb[2][:, :], ALU.bypass, ALU.subtract)
                    p.op("vector", "tensor_mul", src[:], src[:], p.pb[3][:, :])
                    p.op("vector", "tensor_add", xTc[j][:], xTc[j][:], src[:])
                    p.dma("gpsimd", x1T_dr[j * 128:(j + 1) * 128, cc * CH:(cc + 1) * CH],
                          xTc[j][:])
    p.release(m_half)
    p.release(mark0)



def col_layout(p, cm, vec_dr, n, name=None):
    rows = p.sb([1, 128 * n], F32, side="right")
    cols = p.sb([128, n], F32, name=name)
    p.dma("sync", rows[:], vec_dr.rearrange("(a q) -> a q", q=128 * n) if n > 0 else vec_dr)
    for j in range(n):
        bank = j % 2
        p.op("tensor", "transpose", p.pb[bank][:, 0:1], rows[0:1, j * 128:(j + 1) * 128],
             cm.identf[0:1, 0:1])
        p.op("vector", "tensor_copy", cols[:, j:j + 1], p.pb[bank][:, 0:1])
    return cols


class FeatLN:

    def __init__(self, p, cm, W):
        self.p, self.cm, self.W = p, cm, W
        self.cast = p.sb([128, W], BF16)
        self.rowS = p.sb([1, W], F32)
        self.rowQ = p.sb([1, W], F32)
        self.rowT = p.sb([1, W], F32)

    def stats(self, tiles, b_sum, b_sq, sl=None):
        p, cm, W = self.p, self.cm, self.W
        nd = len(tiles)
        for dt, tl in enumerate(tiles):
            src = tl[:, sl] if sl is not None else tl[:]
            p.op("scalar", "activation", out=self.cast[:], in_=src, func=AF.Copy)
            p.op("tensor", "matmul", p.pb[b_sum][0:1, :W], cm.ones_col_b[:], self.cast[:],
                 start=(dt == 0), stop=(dt == nd - 1), skip_group_check=True)
            p.op("scalar", "activation", out=self.cast[:], in_=src, func=AF.Square)
            p.op("tensor", "matmul", p.pb[b_sq][0:1, :W], cm.ones_col_b[:], self.cast[:],
                 start=(dt == 0), stop=(dt == nd - 1), skip_group_check=True)
        p.op("vector", "tensor_scalar", self.rowS[:], p.pb[b_sum][0:1, :W],
             1.0 / 1024.0, None, ALU.mult)
        p.op("vector", "tensor_scalar", self.rowQ[:], p.pb[b_sq][0:1, :W],
             1.0 / 1024.0, None, ALU.mult)
        p.op("vector", "tensor_mul", self.rowT[:], self.rowS[:], self.rowS[:])
        p.op("vector", "tensor_sub", self.rowQ[:], self.rowQ[:], self.rowT[:])

    def rstd_plain(self, eps_i):
        p, cm = self.p, self.cm
        p.op("scalar", "activation", out=self.rowQ[:], in_=self.rowQ[:], func=AF.Ln,
             bias=cm.c1(eps_i))
        p.op("scalar", "activation", out=self.rowQ[:], in_=self.rowQ[:], func=AF.Exp,
             scale=cm.c1(4))

    def bcast(self, b_mu, b_rs):
        p, cm, W = self.p, self.cm, self.W
        p.op("tensor", "matmul", p.pb[b_mu][:, :W], cm.ones_row_f[:], self.rowS[:],
             start=True, stop=True)
        p.op("tensor", "matmul", p.pb[b_rs][:, :W], cm.ones_row_f[:], self.rowQ[:],
             start=True, stop=True)

    def apply_plain(self, tiles_in, tiles_out, b_mu, b_rs, tmp, sl=None, add_to=None):
        p = self.p
        for dt, (ti, to) in enumerate(zip(tiles_in, tiles_out)):
            src = ti[:, sl] if sl is not None else ti[:]
            p.op("vector", "scalar_tensor_tensor", tmp[:], src, 1.0,
                 p.pb[b_mu][:, :self.W], ALU.bypass, ALU.subtract)
            if add_to is None:
                p.op("vector", "tensor_mul", to, tmp[:], p.pb[b_rs][:, :self.W])
            else:
                p.op("vector", "tensor_mul", tmp[:], tmp[:], p.pb[b_rs][:, :self.W])
                p.op("vector", "tensor_add", to, tmp[:], add_to[dt][:])



def stage_b(p: Prog, cm: CommonSB, x1T_dr, gam_dr, bet_dr, q1_dr, s1_dr, b1_dr,
            q2_dr, s2_dr, b2_dr, x2T_dr):
    F1 = 4096
    CHB = 256
    NCHB = T // CHB
    mark0 = p.mark()
    w1b = [p.sb([128, F1], BF16, name=f"w1b{g}") for g in range(8)]
    w2b = [p.sb([128, D], BF16, name=f"w2b{g}") for g in range(32)]

    m_dq = p.mark()
    qst = [p.sb([128, F1], I32)]
    qf = p.sb([128, F1], F32)
    srow = p.sb([1, F1], F32)
    srep = p.sb([128, F1], F32)
    for g in range(8):
        p.dma("scalar", qst[0][:, 0:2048], q1_dr[g * 128:(g + 1) * 128, 0:2048])
        p.dma("scalar", qst[0][:, 2048:F1], q1_dr[g * 128:(g + 1) * 128, 2048:F1])
        p.dma("scalar", srow[:], s1_dr[g:g + 1, :])
        for h4 in range(8):
            sli = slice(h4 * 512, h4 * 512 + 512)
            p.op("tensor", "matmul", p.pb[h4 % 2][:, :], cm.ones_row_f[:], srow[:, sli],
                 start=True, stop=True)
            p.op("vector", "tensor_copy", srep[:, sli], p.pb[h4 % 2][:, :])
        p.op("gpsimd", "tensor_copy", qf[:], qst[0][:])
        p.op("vector", "scalar_tensor_tensor", w1b[g][:], qf[:], -128.0, srep[:],
             ALU.add, ALU.mult)
    for g in range(32):
        p.dma("scalar", qst[0][:, 0:D], q2_dr[g * 128:(g + 1) * 128, :])
        p.dma("scalar", srow[:, 0:D], s2_dr[g:g + 1, :])
        for h4 in range(2):
            sli = slice(h4 * 512, h4 * 512 + 512)
            p.op("tensor", "matmul", p.pb[h4 % 2][:, :], cm.ones_row_f[:], srow[:, sli],
                 start=True, stop=True)
            p.op("vector", "tensor_copy", srep[:, sli], p.pb[h4 % 2][:, :])
        p.op("gpsimd", "tensor_copy", qf[:, 0:D], qst[0][:, 0:D])
        p.op("vector", "scalar_tensor_tensor", w2b[g][:], qf[:, 0:D], -128.0, srep[:, 0:D],
             ALU.add, ALU.mult)
    p.release(m_dq)

    m_right = p.mark("right")
    gcol = col_layout(p, cm, gam_dr, ND)
    bcol = col_layout(p, cm, bet_dr, ND)
    b2col = col_layout(p, cm, b2_dr, ND)
    hb2col = p.sb([128, ND], F32)
    p.op("vector", "tensor_scalar", hb2col[:], b2col[:], 0.5, None, ALU.mult)
    b1row = p.sb([1, F1], F32, side="right")
    b1cols = p.sb([128, 32], F32)
    p.dma("sync", b1row[:], b1_dr[:])
    for j in range(32):
        p.op("tensor", "transpose", p.pb[j % 2][:, 0:1], b1row[:, j * 128:(j + 1) * 128],
             cm.identf[0:1, 0:1])
        p.op("vector", "tensor_copy", b1cols[:, j:j + 1], p.pb[j % 2][:, 0:1])
    p.release(m_right, "right")

    x1c = [p.sb([128, CHB], F32) for _ in range(ND)]
    xnc = [p.sb([128, CHB], F32) for _ in range(ND)]
    hb = [p.sb([128, CHB], BF16) for _ in range(ND)]
    yb = [p.sb([128, CHB], BF16) for _ in range(32)]
    tmp = p.sb([128, CHB], F32)
    tmp2 = tmp
    ln = FeatLN(p, cm, CHB)

    for cc in range(NCHB):
        tsl = slice(cc * CHB, (cc + 1) * CHB)
        for j in range(ND):
            p.dma("sync", x1c[j][:], x1T_dr[j * 128:(j + 1) * 128, tsl])
        ln.stats(x1c, 0, 1)
        ln.rstd_plain(2)
        ln.bcast(0, 1)
        ln.apply_plain(x1c, [t[:] for t in xnc], 0, 1, tmp)
        ln.stats(xnc, 2, 3)
        ln.rstd_plain(2)
        ln.bcast(2, 3)
        for dt in range(ND):
            p.op("vector", "scalar_tensor_tensor", tmp[:], xnc[dt][:], 1.0,
                 p.pb[2][:, :CHB], ALU.bypass, ALU.subtract)
            p.op("vector", "tensor_mul", tmp[:], tmp[:], p.pb[3][:, :CHB])
            p.op("vector", "tensor_scalar", hb[dt][:], tmp[:], gcol[:, dt:dt + 1],
                 bcol[:, dt:dt + 1], ALU.mult, ALU.add)
        for ft in range(32):
            bank = 4 + ft % 2
            for din in range(ND):
                p.op("tensor", "matmul", p.pb[bank][:, :CHB],
                     w1b[din][:, ft * 128:(ft + 1) * 128], hb[din][:],
                     start=(din == 0), stop=(din == ND - 1))
            p.op("scalar", "activation", out=yb[ft][:], in_=p.pb[bank][:, :CHB],
                 func=AF.Silu, bias=b1cols[:, ft:ft + 1])
        for dout in range(ND):
            bank = 6 + dout % 2
            for ft in range(32):
                p.op("tensor", "matmul", p.pb[bank][:, :CHB],
                     w2b[ft][:, dout * 128:(dout + 1) * 128], yb[ft][:],
                     start=(ft == 0), stop=(ft == 31))
            p.op("vector", "tensor_scalar", tmp2[:], p.pb[bank][:, :CHB], 0.5,
                 hb2col[:, dout:dout + 1], ALU.mult, ALU.add)
            p.op("vector", "tensor_add", xnc[dout][:], xnc[dout][:], tmp2[:])
        ln.stats(xnc, 0, 1)
        ln.rstd_plain(2)
        ln.bcast(0, 1)
        for dt in range(ND):
            p.op("vector", "scalar_tensor_tensor", tmp[:], xnc[dt][:], 1.0,
                 p.pb[0][:, :CHB], ALU.bypass, ALU.subtract)
            p.op("vector", "tensor_mul", tmp[:], tmp[:], p.pb[1][:, :CHB])
            p.op("vector", "tensor_add", x1c[dt][:], x1c[dt][:], tmp[:])
            p.dma("gpsimd", x2T_dr[dt * 128:(dt + 1) * 128, tsl], x1c[dt][:])
    p.release(mark0)



def build_stage_a_program():
    nc = bass.Bass("TRN2", target_bir_lowering=False)
    dr = declare_const_inputs(nc)
    x = nc.dram_tensor("xa", [T, D], F32, kind="ExternalInput")
    ls = nc.dram_tensor("l0_ls", [1, D], F32, kind="ExternalInput")
    lb = nc.dram_tensor("l0_lb", [1, D], F32, kind="ExternalInput")
    wq = nc.dram_tensor("l0_wq", [D, D], F32, kind="ExternalInput")
    wk = nc.dram_tensor("l0_wk", [D, D], F32, kind="ExternalInput")
    wv = nc.dram_tensor("l0_wv", [D, D], F32, kind="ExternalInput")
    x1T = nc.dram_tensor("x1T", [D, T], F32, kind="ExternalOutput")
    scr = nc.dram_tensor("scrA", [514, T], F32, kind="Internal")
    p = Prog(nc)
    cm = CommonSB(p, dr)
    stage_a(p, cm, x, ls, lb, wq, wk, wv, x1T, scr)
    p.barrier()
    p.finalize()
    return nc



def stage_c(p: Prog, cm: CommonSB, x2T_dr, xpw_dr, dtw_dr, dtb_dr, alog_dr,
            dpar_dr, lna_dr, lnb_dr, selb_dr, pf_dr, rb_dr, x3T_dr):
    TC = 512
    NTC = T // TC
    mark0 = p.mark()

    m_right = p.mark("right")
    xpw_st = p.sb([96, D], F32, side="right")
    xpw_b = p.sb([96, D], BF16, side="right")
    dtw_st = p.sb([128, 64], F32, side="right")
    dtw_b = p.sb([128, 64], BF16, side="right")
    alog_st = p.sb([128, 128], F32, side="right")

    xpwT = [p.sb([128, 96], BF16) for _ in range(ND)]
    dtwT = [p.sb([64, 128], BF16) for _ in range(ND)]
    negA = p.sb([128, 128], F32)
    selb = p.sb([128, 16 * 128], BF16)
    pf32 = p.sb([128, 16 * 128], F32)
    rb = p.sb([16, 128], BF16)
    state = p.sb([128, 128], F32)

    p.dma("sync", xpw_st[:], xpw_dr[:])
    p.op("scalar", "activation", out=xpw_b[:], in_=xpw_st[:], func=AF.Copy)
    for din in range(ND):
        bank = din % 2
        p.op("tensor", "transpose", p.pbh[bank][:, 0:96],
             xpw_b[:, din * 128:(din + 1) * 128], cm.identb[0:96, 0:96])
        p.op("vector", "tensor_copy", xpwT[din][:], p.pbh[bank][:, 0:96])
    for dt in range(ND):
        bank = dt % 2
        p.dma("sync", dtw_st[:], dtw_dr[dt * 128:(dt + 1) * 128, :])
        p.op("scalar", "activation", out=dtw_b[:], in_=dtw_st[:], func=AF.Copy)
        p.op("tensor", "transpose", p.pbh[bank][0:64, 0:128], dtw_b[:], cm.identb[:])
        p.op("vector", "tensor_copy", dtwT[dt][:], p.pbh[bank][0:64, 0:128])
    p.dma("sync", alog_st[:], alog_dr.rearrange("(k x) n -> k (x n)", x=8))
    p.op("tensor", "transpose", p.pb[0][:, 0:128], alog_st[:], cm.identf[:])
    p.op("scalar", "activation", out=negA[:], in_=p.pb[0][:, 0:128], func=AF.Exp)
    p.op("vector", "tensor_scalar", negA[:], negA[:], -1.0, None, ALU.mult)
    p.op("vector", "memset", state[:], 0.0)
    for j in range(16):
        p.dma("sync", selb[:, j * 128:(j + 1) * 128], selb_dr[j * 128:(j + 1) * 128, :])
        p.dma("sync", pf32[:, j * 128:(j + 1) * 128], pf_dr[j * 128:(j + 1) * 128, :])
    p.dma("sync", rb[:], rb_dr[:])
    dcol = col_layout(p, cm, dpar_dr, ND)
    dtbcol = col_layout(p, cm, dtb_dr, ND)
    acol = col_layout(p, cm, lna_dr, ND)
    bcol = col_layout(p, cm, lnb_dr, ND)
    p.release(m_right, "right")

    x2c = [p.sb([128, TC], F32) for _ in range(ND)]
    xln = [p.sb([128, TC], F32) for _ in range(ND)]
    xln_b = [p.sb([128, TC], BF16) for _ in range(ND)]
    delta = [p.sb([128, TC], F32) for _ in range(ND)]
    dhi = [p.sb([128, TC], BF16) for _ in range(ND)]
    dlo = [p.sb([128, TC], BF16) for _ in range(ND)]
    uhi = [p.sb([128, TC], BF16) for _ in range(ND)]
    ulo = [p.sb([128, TC], BF16) for _ in range(ND)]
    u_f = p.sb([128, TC], F32)
    drhi = p.sb([64, TC], BF16)
    drlo = p.sb([64, TC], BF16)
    bm_f = p.sb([16, TC], F32)
    bmhi = p.sb([16, TC], BF16)
    bmlo = p.sb([16, TC], BF16)
    cmhi = p.sb([16, TC], BF16)
    cmlo = p.sb([16, TC], BF16)
    bmrep = p.sb([128, TC], F32)
    cmrep = p.sb([128, TC], F32)
    sp1 = [p.sb([128, TC], F32) for _ in range(2)]
    dA = [p.sb([128, TC], F32) for _ in range(3)]
    u_sb = [p.sb([128, TC], F32) for _ in range(3)]
    bx = [p.sb([128, TC], F32) for _ in range(3)]
    hh = [p.sb([128, TC], F32) for _ in range(3)]
    hc = [p.sb([128, TC], F32) for _ in range(3)]
    y_sb = [p.sb([128, TC], F32) for _ in range(ND)]
    sso = [p.sb([128, TC], F32) for _ in range(ND)]
    tmp = p.sb([128, TC], F32)
    ln = FeatLN(p, cm, TC)

    for cc in range(NTC):
        tsl = slice(cc * TC, (cc + 1) * TC)
        for j in range(ND):
            p.dma("sync", x2c[j][:], x2T_dr[j * 128:(j + 1) * 128, tsl])
        ln.stats(x2c, 0, 1)
        ln.rstd_plain(2)
        ln.bcast(0, 1)
        ln.apply_plain(x2c, [t[:] for t in xln], 0, 1, tmp)
        for j in range(ND):
            p.op("scalar", "activation", out=xln_b[j][:], in_=xln[j][:], func=AF.Copy)
        for din in range(ND):
            st = (din == 0)
            sp = (din == ND - 1)
            p.op("tensor", "matmul", p.pb[0][0:64, :], xpwT[din][:, 0:64], xln_b[din][:],
                 start=st, stop=sp)
            p.op("tensor", "matmul", p.pb[1][0:16, :], xpwT[din][:, 64:80], xln_b[din][:],
                 start=st, stop=sp, skip_group_check=True)
            p.op("tensor", "matmul", p.pb[2][0:16, :], xpwT[din][:, 80:96], xln_b[din][:],
                 start=st, stop=sp, skip_group_check=True)
        p.op("scalar", "activation", out=drhi[:], in_=p.pb[0][0:64, :], func=AF.Copy)
        p.op("vector", "tensor_sub", drlo[:], p.pb[0][0:64, :], drhi[:])
        p.op("scalar", "activation", out=bmhi[:], in_=p.pb[1][0:16, :], func=AF.Copy)
        p.op("vector", "tensor_sub", bmlo[:], p.pb[1][0:16, :], bmhi[:])
        p.op("scalar", "activation", out=cmhi[:], in_=p.pb[2][0:16, :], func=AF.Copy)
        p.op("vector", "tensor_sub", cmlo[:], p.pb[2][0:16, :], cmhi[:])
        p.op("tensor", "matmul", p.pb[5][:, :], rb[:], bmhi[:], start=True, stop=False)
        p.op("tensor", "matmul", p.pb[5][:, :], rb[:], bmlo[:], start=False, stop=True)
        p.op("vector", "tensor_copy", bmrep[:], p.pb[5][:, :])
        p.op("tensor", "matmul", p.pb[5][:, :], rb[:], cmhi[:], start=True, stop=False)
        p.op("tensor", "matmul", p.pb[5][:, :], rb[:], cmlo[:], start=False, stop=True)
        p.op("vector", "tensor_copy", cmrep[:], p.pb[5][:, :])
        for dt in range(ND):
            bank = 3 + dt % 2
            p.op("tensor", "matmul", p.pb[bank][:, :], dtwT[dt][:], drhi[:],
                 start=True, stop=False)
            p.op("tensor", "matmul", p.pb[bank][:, :], dtwT[dt][:], drlo[:],
                 start=False, stop=True)
            s1 = sp1[dt % 2]
            p.op("scalar", "activation", out=s1[:], in_=p.pb[bank][:, :], func=AF.Abs,
                 bias=dtbcol[:, dt:dt + 1])
            p.op("scalar", "activation", out=s1[:], in_=s1[:], func=AF.Exp, scale=cm.c(8))
            p.op("scalar", "activation", out=s1[:], in_=s1[:], func=AF.Ln, bias=cm.c(5))
            p.op("scalar", "activation", out=delta[dt][:], in_=p.pb[bank][:, :], func=AF.Relu,
                 bias=dtbcol[:, dt:dt + 1])
            p.op("vector", "tensor_add", delta[dt][:], delta[dt][:], s1[:])
            p.op("scalar", "activation", out=dhi[dt][:], in_=delta[dt][:], func=AF.Copy)
            p.op("vector", "tensor_sub", dlo[dt][:], delta[dt][:], dhi[dt][:])
            p.op("vector", "tensor_mul", u_f[:], delta[dt][:], xln[dt][:])
            p.op("scalar", "activation", out=uhi[dt][:], in_=u_f[:], func=AF.Copy)
            p.op("vector", "tensor_sub", ulo[dt][:], u_f[:], uhi[dt][:])
        for k in range(128):
            dt, jj = k // 16, k % 16
            bdr = k % 2
            bur = 2 + k % 2
            by = 6 + dt % 2
            sel = selb[:, jj * 128:(jj + 1) * 128]
            p.op("tensor", "matmul", p.pb[bdr][:, :], sel, dhi[dt][:], start=True, stop=False)
            p.op("tensor", "matmul", p.pb[bdr][:, :], sel, dlo[dt][:], start=False, stop=True)
            p.op("scalar", "activation", out=dA[k % 3][:], in_=p.pb[bdr][:, :], func=AF.Exp,
                 scale=negA[:, k:k + 1])
            p.op("tensor", "matmul", p.pb[bur][:, :], sel, uhi[dt][:], start=True, stop=False)
            p.op("tensor", "matmul", p.pb[bur][:, :], sel, ulo[dt][:], start=False, stop=True)
            p.op("scalar", "activation", out=u_sb[k % 3][:], in_=p.pb[bur][:, :], func=AF.Copy)
            p.op("gpsimd", "tensor_mul", bx[k % 3][:], u_sb[k % 3][:], bmrep[:])
            p.op("vector", "tensor_tensor_scan", hh[k % 3][:], dA[k % 3][:], bx[k % 3][:],
                 state[:, k:k + 1], ALU.mult, ALU.add)
            p.op("vector", "tensor_copy", state[:, k:k + 1], hh[k % 3][:, TC - 1:TC])
            p.op("vector", "tensor_mul", hc[k % 3][:], hh[k % 3][:], cmrep[:])
            p.op("tensor", "matmul", p.pb[by][:, :], pf32[:, jj * 128:(jj + 1) * 128],
                 hc[k % 3][:], start=(jj == 0), stop=(jj == 15))
            if jj == 15:
                p.op("vector", "tensor_scalar", tmp[:], xln[dt][:], dcol[:, dt:dt + 1],
                     None, ALU.mult)
                p.op("vector", "tensor_add", y_sb[dt][:], p.pb[by][:, :], tmp[:])
        ln.stats(y_sb, 0, 1)
        p.op("vector", "tensor_scalar", ln.rowQ[:], ln.rowQ[:],
             1024.0 / 1023.0, None, ALU.mult)
        p.op("scalar", "activation", out=ln.rowQ[:], in_=ln.rowQ[:], func=AF.Ln)
        p.op("scalar", "activation", out=ln.rowQ[:], in_=ln.rowQ[:], func=AF.Exp,
             scale=cm.c1(9))
        p.op("vector", "tensor_scalar", ln.rowQ[:], ln.rowQ[:], 1e-12, None, ALU.add)
        p.op("scalar", "activation", out=ln.rowQ[:], in_=ln.rowQ[:], func=AF.Ln)
        p.op("scalar", "activation", out=ln.rowQ[:], in_=ln.rowQ[:], func=AF.Exp,
             scale=cm.c1(8))
        ln.bcast(0, 1)
        for dt in range(ND):
            p.op("vector", "scalar_tensor_tensor", tmp[:], y_sb[dt][:], 1.0,
                 p.pb[0][:, :], ALU.bypass, ALU.subtract)
            p.op("vector", "tensor_mul", tmp[:], tmp[:], p.pb[1][:, :])
            p.op("vector", "tensor_scalar", sso[dt][:], tmp[:], acol[:, dt:dt + 1],
                 bcol[:, dt:dt + 1], ALU.mult, ALU.add)
        ln.stats(sso, 2, 3)
        ln.rstd_plain(2)
        ln.bcast(2, 3)
        for dt in range(ND):
            p.op("vector", "scalar_tensor_tensor", tmp[:], sso[dt][:], 1.0,
                 p.pb[2][:, :], ALU.bypass, ALU.subtract)
            p.op("vector", "tensor_mul", tmp[:], tmp[:], p.pb[3][:, :])
            p.op("vector", "tensor_add", x2c[dt][:], x2c[dt][:], tmp[:])
            p.dma("gpsimd", x3T_dr[dt * 128:(dt + 1) * 128, tsl], x2c[dt][:])
    p.release(mark0)


def host_ssm_consts():
    sel = np.zeros((16, 128, 128), np.float32)
    for j in range(16):
        for pp in range(128):
            sel[j, 8 * j + pp // 16, pp] = 1.0
    pf = np.transpose(sel, (0, 2, 1)).reshape(2048, 128).astype(np.float32)
    selb = sel.reshape(2048, 128).astype(BF)
    rbm = np.zeros((16, 128), np.float32)
    for pp in range(128):
        rbm[pp % 16, pp] = 1.0
    return {"selb": selb, "pf": pf, "rb": rbm.astype(BF)}



W1F = 5460
F2 = 2730
NF2 = 22
CHA = 512
CHF = 256


def f2w(i):
    return min(128, F2 - 128 * i)


def hb2ftb(cm, p):
    return p.hb2ftb_t[:]


def col_layout2(p, cm, vec_ap, n, name=None):
    rows = p.sb([1, 128 * n], F32, side="right")
    cols = p.sb([128, n], F32, name=name)
    p.dma("sync", rows[:], vec_ap)
    for j in range(n):
        bank = j % 2
        p.op("tensor", "transpose", p.pb[bank][:, 0:1], rows[0:1, j * 128:(j + 1) * 128],
             cm.identf[0:1, 0:1])
        p.op("vector", "tensor_copy", cols[:, j:j + 1], p.pb[bank][:, 0:1])
    return cols


def stage_d(p: Prog, cm: CommonSB, dd, x3T_dr, y_dr, ping_dr, pong_dr, nlayers=6):
    mark0 = p.mark()
    masks = p.sb([128, 4 * 256], BF16)
    hb2b = p.sb([128, 2], BF16)
    hb2f = p.sb([128, 2], F32)
    hb2ft = p.sb([2, 128], F32)
    hb2ftb_t = p.sb([2, 128], BF16)
    for i in range(4):
        p.dma("sync", masks[:, i * 256:(i + 1) * 256], dd["masks"][i * 128:(i + 1) * 128, :])
    p.dma("sync", hb2b[:], dd["hb2b"][:])
    p.dma("sync", hb2f[:], dd["hb2f"][:])
    p.dma("sync", hb2ft[:], dd["hb2ft"][:])
    p.dma("sync", hb2ftb_t[:], dd["hb2ftb"][:])
    p.hb2ftb_t = hb2ftb_t

    m_pre = p.mark()
    xc = [p.sb([128, CHA], F32) for _ in range(ND)]
    tmp0 = p.sb([128, CHA], F32)
    lnp = FeatLN(p, cm, CHA)
    for cc in range(NCH):
        tsl = slice(cc * CHA, (cc + 1) * CHA)
        for j in range(ND):
            p.dma("sync", xc[j][:], x3T_dr[j * 128:(j + 1) * 128, tsl])
        lnp.stats(xc, 0, 1)
        lnp.rstd_plain(2)
        lnp.bcast(0, 1)
        lnp.apply_plain(xc, [t[:] for t in xc], 0, 1, tmp0)
        for j in range(ND):
            p.dma("gpsimd", ping_dr[j * 128:(j + 1) * 128, tsl], xc[j][:])
    p.release(m_pre)

    src_dr, dst_dr = ping_dr, pong_dr
    for ll in range(nlayers):
        attn_sublayer(p, cm, dd, ll, src_dr, dst_dr, masks, hb2f, hb2ft)
        src_dr, dst_dr = dst_dr, src_dr
        ffn_sublayer(p, cm, dd, ll, src_dr, dst_dr)
        src_dr, dst_dr = dst_dr, src_dr

    m_post = p.mark()
    xc = [p.sb([128, CHA], F32) for _ in range(ND)]
    sc = [p.sb([128, CHA], F32) for _ in range(ND)]
    otile = [p.sb([128, D], F32) for _ in range(2)]
    tmp0 = p.sb([128, CHA], F32)
    lnp = FeatLN(p, cm, CHA)
    for cc in range(NCH):
        tsl = slice(cc * CHA, (cc + 1) * CHA)
        for j in range(ND):
            p.dma("sync", xc[j][:], x3T_dr[j * 128:(j + 1) * 128, tsl])
            p.dma("sync", sc[j][:], src_dr[j * 128:(j + 1) * 128, tsl])
        lnp.stats(sc, 0, 1)
        lnp.rstd_plain(2)
        lnp.bcast(0, 1)
        lnp.apply_plain(sc, [t[:] for t in sc], 0, 1, tmp0, add_to=xc)
        for q4 in range(4):
            ot = otile[q4 % 2]
            for j in range(ND):
                bank = 2 + (q4 * ND + j) % 2
                p.op("tensor", "transpose", p.pb[bank][:, 0:128],
                     sc[j][:, q4 * 128:(q4 + 1) * 128], cm.identf[:])
                p.op("vector", "tensor_copy", ot[:, j * 128:(j + 1) * 128],
                     p.pb[bank][:, 0:128])
            p.dma("gpsimd", y_dr[cc * CHA + q4 * 128: cc * CHA + (q4 + 1) * 128, :], ot[:])
    p.release(m_post)
    p.release(mark0)


def attn_sublayer(p, cm, dd, ll, src_dr, dst_dr, masks, hb2f, hb2ft):
    CH2 = 256
    NC2 = T // CH2
    m_w = p.mark()
    wst = [p.sb([128, 512], F32) for _ in range(4)]
    wqkvb = [p.sb([128, 3072], BF16) for _ in range(ND)]
    wob = [p.sb([128, D], BF16) for _ in range(ND)]
    it = 0
    for s6 in range(6):
        for din in range(ND):
            p.dma("scalar", wst[it % 4][:], dd["wqkv"][ll, din * 128:(din + 1) * 128,
                                                       s6 * 512:(s6 + 1) * 512])
            p.op("gpsimd", "tensor_copy", wqkvb[din][:, s6 * 512:(s6 + 1) * 512],
                 wst[it % 4][:])
            it += 1
    for s2 in range(2):
        for din in range(ND):
            p.dma("scalar", wst[it % 4][:], dd["wo"][ll, din * 128:(din + 1) * 128,
                                                     s2 * 512:(s2 + 1) * 512])
            p.op("gpsimd", "tensor_copy", wob[din][:, s2 * 512:(s2 + 1) * 512],
                 wst[it % 4][:])
            it += 1
    qsr = p.sb([1, 64], F32)
    ksr = p.sb([1, 64], F32)
    qsr_s = p.sb([1, 64], F32)
    ksr_s = p.sb([1, 64], F32)
    r1 = p.sb([1, 64], F32)
    r2 = p.sb([1, 1], F32)
    qscol = p.sb([128, 1], F32)
    kscol = p.sb([128, 1], F32)
    qscol_s = p.sb([128, 1], F32)
    kscol_s = p.sb([128, 1], F32)
    shcol = p.sb([128, 1], F32)
    p.dma("sync", qsr[:], dd["qs"][ll:ll + 1, :])
    p.dma("sync", ksr[:], dd["ks"][ll:ll + 1, :])
    for sr, dsr in ((qsr, qsr_s), (ksr, ksr_s)):
        p.op("vector", "tensor_copy", dsr[:, 0:32], sr[:, 32:64])
        p.op("vector", "tensor_copy", dsr[:, 32:64], sr[:, 0:32])
    for dst_c, src_r in ((qscol, qsr), (kscol, ksr), (qscol_s, qsr_s), (kscol_s, ksr_s)):
        p.op("tensor", "transpose", p.pb[6][0:64, 0:1], src_r[:], cm.identf[0:1, 0:1])
        p.op("vector", "tensor_copy", dst_c[0:64, 0:1], p.pb[6][0:64, 0:1])
        p.op("vector", "tensor_copy", dst_c[64:128, 0:1], p.pb[6][0:64, 0:1])
    p.op("vector", "tensor_mul", r1[:], qsr[:], ksr[:])
    p.op("vector", "reduce_max", r2[:], r1[:], axis=AX.X, apply_absolute_value=True)
    p.op("vector", "tensor_scalar", r2[:], r2[:], -8.0, 80.0, ALU.mult, ALU.add)
    p.op("vector", "tensor_scalar_min", r2[:], r2[:], 0.0)
    p.op("tensor", "matmul", p.pb[6][:, 0:1], cm.ones_row_f[:], r2[:], start=True, stop=True)
    p.op("vector", "tensor_copy", shcol[:], p.pb[6][:, 0:1])

    xbuf = [[p.sb([128, CH2 + 1], F32) for _ in range(ND)] for _ in range(2)]
    xb_b = [[p.sb([128, CH2 + 1], BF16) for _ in range(ND)] for _ in range(2)]
    rot_f = p.sb([128, 8 * CH2], F32)
    rot_sh = [p.sb([128, CH2], F32) for _ in range(2)]
    rrep_q = p.sb([128, 8 * CH2], F32)
    q_b = [p.sb([128, 8 * CH2], BF16) for _ in range(2)]
    qp = [p.sb([128, CH2], BF16) for _ in range(2)]
    k_b = [p.sb([128, 8 * CH2], BF16) for _ in range(3)]
    att_b = p.sb([128, 8 * CH2], BF16)
    V_sb = [[p.sb([128, 16 * 128], BF16) for _ in range(2)] for _ in range(3)]
    tabs = [p.sb([128, CH2], F32) for _ in range(4)]
    rows2 = p.sb([2, CH2], F32)
    rows2h = p.sb([2, CH2], BF16)
    rows2l = p.sb([2, CH2], BF16)
    sq_f = p.sb([128, CH2], F32)
    tmpA = p.sb([128, CH2], F32)
    vt_b = p.sb([128, CH2], BF16)
    probs2 = [[p.sb([128, 512], BF16) for _ in range(2)] for _ in range(2)]
    den_r = [p.sb([1, CH2], F32) for _ in range(2)]
    den_h = [p.sb([1, CH2], BF16) for _ in range(2)]
    den_l = [p.sb([1, CH2], BF16) for _ in range(2)]

    def emit_front(cc):
        par2 = cc % 2
        par3 = cc % 3
        t0 = cc * CH2
        for j in range(ND):
            if cc == 0:
                p.op("vector", "memset", xbuf[par2][j][:, 0:1], 0.0)
                p.dma("sync", xbuf[par2][j][:, 1:CH2 + 1], src_dr[j * 128:(j + 1) * 128, 0:CH2])
            else:
                p.dma("sync", xbuf[par2][j][:], src_dr[j * 128:(j + 1) * 128, t0 - 1:t0 + CH2])
            p.op("scalar", "activation", out=xb_b[par2][j][:], in_=xbuf[par2][j][:],
                 func=AF.Copy)

        def rhs(din):
            return (xb_b[par2][din][:, 1:CH2 + 1] if din < 4
                    else xb_b[par2][din][:, 0:CH2])

        for ti, key, col in ((0, "tab_cq", qscol), (1, "tab_sq", qscol_s),
                             (2, "tab_ck", kscol), (3, "tab_sk", kscol_s)):
            p.dma("sync", tabs[ti][:], dd[key][:, t0:t0 + CH2])
            p.op("vector", "tensor_scalar", tabs[ti][:], tabs[ti][:], col[:, 0:1],
                 None, ALU.mult)

        for is_k in (False, True):
            coff = 1024 if is_k else 0
            for dt in range(ND):
                bank = 6 + dt % 2
                csl = slice(dt * CH2, (dt + 1) * CH2)
                for din in range(ND):
                    p.op("tensor", "matmul", p.pb[bank][:, 0:CH2],
                         wqkvb[din][:, coff + dt * 128: coff + (dt + 1) * 128],
                         rhs(din), start=(din == 0), stop=(din == ND - 1))
                p.op("scalar", "activation", out=rot_f[:, csl], in_=p.pb[bank][:, 0:CH2],
                     func=AF.Copy)
                p.op("scalar", "activation", out=sq_f[:], in_=p.pb[bank][:, 0:CH2],
                     func=AF.Square)
                p.op("tensor", "matmul", p.pb[bank][0:2, 256:256 + CH2], hb2f[:], sq_f[:],
                     start=True, stop=True, skip_group_check=True)
                p.op("vector", "tensor_scalar_max", rows2[:], p.pb[bank][0:2, 256:256 + CH2],
                     1e-24)
                p.op("scalar", "activation", out=rows2[:], in_=rows2[:], func=AF.Ln)
                p.op("scalar", "activation", out=rows2[:], in_=rows2[:], func=AF.Exp,
                     scale=cm.cst[0:2, 4:5])
                if is_k:
                    p.op("vector", "tensor_scalar", rows2[:], rows2[:], 8.0, None, ALU.mult)
                p.op("scalar", "activation", out=rows2h[:], in_=rows2[:], func=AF.Copy)
                p.op("vector", "tensor_sub", rows2l[:], rows2[:], rows2h[:])
                p.op("tensor", "matmul", p.pb[bank][:, 256:256 + CH2], hb2ftb(cm, p), rows2h[:],
                     start=True, stop=False, skip_group_check=True)
                p.op("tensor", "matmul", p.pb[bank][:, 256:256 + CH2], hb2ftb(cm, p), rows2l[:],
                     start=False, stop=True, skip_group_check=True)
                p.op("vector", "tensor_copy", rrep_q[:, csl], p.pb[bank][:, 256:256 + CH2])
            def shuf(dt):
                csl = slice(dt * CH2, (dt + 1) * CH2)
                for a in range(2):
                    for b_ in range(2):
                        p.dma("scalar",
                              rot_sh[dt % 2][a * 64 + b_ * 32: a * 64 + b_ * 32 + 32, :],
                              rot_f[a * 64 + (1 - b_) * 32: a * 64 + (1 - b_) * 32 + 32, csl])
            shuf(0)
            ct, st_ = (tabs[2], tabs[3]) if is_k else (tabs[0], tabs[1])
            for dt in range(ND):
                if dt + 1 < ND:
                    shuf(dt + 1)
                csl = slice(dt * CH2, (dt + 1) * CH2)
                p.op("vector", "tensor_mul", tmpA[:], rot_f[:, csl], ct[:])
                p.op("vector", "tensor_mul", rot_f[:, csl], rot_sh[dt % 2][:], st_[:])
                p.op("vector", "tensor_add", rot_f[:, csl], rot_f[:, csl], tmpA[:])
                dst_qb = k_b[par3] if is_k else q_b[par2]
                p.op("vector", "tensor_mul", dst_qb[:, csl], rot_f[:, csl],
                     rrep_q[:, csl])
        for dt in range(ND):
            bank = 6 + dt % 2
            for din in range(ND):
                p.op("tensor", "matmul", p.pb[bank][:, 0:CH2],
                     wqkvb[din][:, 2048 + dt * 128: 2048 + (dt + 1) * 128],
                     rhs(din), start=(din == 0), stop=(din == ND - 1))
            p.op("scalar", "activation", out=vt_b[:], in_=p.pb[bank][:, 0:CH2], func=AF.Copy)
            for tb in range(2):
                tbk = 6 + (dt * 2 + tb + 1) % 2
                p.op("tensor", "transpose", p.pbh[tbk][:, 0:128],
                     vt_b[:, tb * 128:(tb + 1) * 128], cm.identb[:])
                vv = V_sb[par3][tb][:].rearrange("p (h c) -> p h c", c=128)
                p.op("vector", "tensor_copy", vv[:, 2 * dt:2 * dt + 2, 0:64],
                     p.pbh[tbk][:, 0:128].rearrange("p (h c) -> p h c", c=64))
        for tb in range(2):
            vv = V_sb[par3][tb][:].rearrange("p (h c) -> p h c", c=128)
            p.op("vector", "memset", vv[:, :, 64:65], 1.0)
            p.op("vector", "memset", vv[:, :, 65:128], 0.0)

    def emit_attn(cc):
        par2 = cc % 2
        par3 = cc % 3
        prev3 = (cc - 1) % 3
        t0 = cc * CH2
        kjs = []
        for pos in range(4):
            gt = cc * 2 - 2 + pos
            if gt < 0:
                continue
            kjs.append((pos, prev3 if pos < 2 else par3, gt % 2))

        def d_scores(h):
            ht, hb = h // 2, 64 * (h % 2)
            qsl = slice(ht * CH2, (ht + 1) * CH2)
            g0 = (h % 2) * 2
            p.op("vector", "tensor_copy", qp[h % 2][hb:hb + 64, :],
                 q_b[par2][hb:hb + 64, qsl])
            for idx, (pos, kpar, tb) in enumerate(kjs):
                bank = g0 + idx // 2
                off = (idx % 2) * 256
                ksl = slice(ht * CH2 + tb * 128, ht * CH2 + tb * 128 + 128)
                p.op("tensor", "matmul", p.pb[bank][:, off:off + 256],
                     k_b[kpar][:, ksl], qp[h % 2][:],
                     start=(idx % 2 == 0), stop=False, skip_group_check=True)
            for pi in range((len(kjs) + 1) // 2):
                bank = g0 + pi
                pos0 = kjs[2 * pi][0]
                p.op("tensor", "matmul", p.pb[bank][:, :],
                     cm.identb[:], masks[:, pos0 * 256: pos0 * 256 + 512],
                     start=False, stop=True, skip_group_check=True)
                p.op("scalar", "activation", out=probs2[h % 2][pi][:],
                     in_=p.pb[bank][:, :], func=AF.Exp, bias=shcol[:, 0:1])

        def d_pv_tail(h):
            ht, hb = h // 2, 64 * (h % 2)
            pvbank = 4 + h % 2
            for idx, (pos, kpar, tb) in enumerate(kjs):
                off = (idx % 2) * 256
                vv = V_sb[kpar][tb][:].rearrange("p (hx c) -> p hx c", c=128)[:, h, :]
                p.op("tensor", "matmul", p.pb[pvbank][:, 0:CH2], vv,
                     probs2[h % 2][idx // 2][:, off:off + 256],
                     start=(idx == 0), stop=(idx == len(kjs) - 1),
                     skip_group_check=True)
            p.op("scalar", "activation", out=den_r[h % 2][:],
                 in_=p.pb[pvbank][64:65, 0:CH2], func=AF.Ln)
            p.op("scalar", "activation", out=den_r[h % 2][:], in_=den_r[h % 2][:],
                 func=AF.Exp, scale=cm.c1(8))
            asl = slice(ht * CH2, (ht + 1) * CH2)
            p.op("vector", "tensor_copy", att_b[hb:hb + 64, asl],
                 p.pb[pvbank][0:64, 0:CH2])
            p.op("scalar", "activation", out=den_h[h % 2][:], in_=den_r[h % 2][:],
                 func=AF.Copy)
            p.op("vector", "tensor_sub", den_l[h % 2][:], den_r[h % 2][:], den_h[h % 2][:])
            if h % 2 == 1:
                dvb = (h % 2) * 2
                p.op("tensor", "matmul", p.pb[dvb][0:64, 0:CH2],
                     cm.ones_row_b[:, 0:64], den_h[0][:], start=True, stop=False,
                     skip_group_check=True)
                p.op("tensor", "matmul", p.pb[dvb][0:64, 0:CH2],
                     cm.ones_row_b[:, 0:64], den_l[0][:], start=False, stop=True,
                     skip_group_check=True)
                p.op("tensor", "matmul", p.pb[dvb][64:128, 0:CH2],
                     cm.ones_row_b[:, 0:64], den_h[1][:], start=True, stop=False,
                     skip_group_check=True)
                p.op("tensor", "matmul", p.pb[dvb][64:128, 0:CH2],
                     cm.ones_row_b[:, 0:64], den_l[1][:], start=False, stop=True,
                     skip_group_check=True)
                p.op("vector", "tensor_mul", att_b[:, asl], att_b[:, asl],
                     p.pb[dvb][:, 0:CH2])

        d_scores(0)
        for h in range(1, 16):
            d_scores(h)
            d_pv_tail(h - 1)
        d_pv_tail(15)
        for dout in range(ND):
            bank = 6 + dout % 2
            for din in range(ND):
                p.op("tensor", "matmul", p.pb[bank][:, 0:CH2],
                     wob[din][:, dout * 128:(dout + 1) * 128],
                     att_b[:, din * CH2:(din + 1) * CH2],
                     start=(din == 0), stop=(din == ND - 1))
            p.op("vector", "tensor_add", xbuf[par2][dout][:, 1:CH2 + 1],
                 xbuf[par2][dout][:, 1:CH2 + 1], p.pb[bank][:, 0:CH2])
            p.dma("gpsimd", dst_dr[dout * 128:(dout + 1) * 128, t0:t0 + CH2],
                  xbuf[par2][dout][:, 1:CH2 + 1])

    p.op("vector", "memset", qp[0][64:128, :], 0.0)
    p.op("vector", "memset", qp[1][0:64, :], 0.0)
    emit_front(0)
    for cc in range(1, NC2):
        emit_front(cc)
        emit_attn(cc - 1)
    emit_attn(NC2 - 1)
    p.release(m_w)


def ffn_sublayer(p, cm, dd, ll, src_dr, dst_dr):
    m_w = p.mark()
    wst = [p.sb([128, 512], F32) for _ in range(4)]
    w1b = [p.sb([128, W1F], BF16) for _ in range(ND)]
    w2b = [p.sb([128, D], BF16) for _ in range(NF2)]
    it = 0
    for s in range(11):
        wdt = min(512, W1F - 512 * s)
        for din in range(ND):
            p.dma("scalar", wst[it % 4][:, 0:wdt],
                  dd["w1"][ll, din * 128:(din + 1) * 128, 512 * s:512 * s + wdt])
            p.op("gpsimd", "tensor_copy", w1b[din][:, 512 * s:512 * s + wdt],
                 wst[it % 4][:, 0:wdt])
            it += 1
    for i in range(NF2):
        w = f2w(i)
        for s2 in range(2):
            p.dma("scalar", wst[it % 4][0:w, :], dd["w2"][ll, 128 * i:128 * i + w,
                                                          s2 * 512:(s2 + 1) * 512])
            p.op("gpsimd", "tensor_copy", w2b[i][0:w, s2 * 512:(s2 + 1) * 512],
                 wst[it % 4][0:w, :])
            it += 1
    m_right = p.mark("right")
    b1row = p.sb([1, W1F], F32, side="right")
    b1ac = p.sb([128, NF2], F32)
    b1gc = p.sb([128, NF2], F32)
    p.dma("sync", b1row[:], dd["b1"][ll:ll + 1, :])
    for i in range(NF2):
        w = f2w(i)
        p.op("tensor", "transpose", p.pb[6][0:w, 0:1], b1row[0:1, 128 * i:128 * i + w],
             cm.identf[0:1, 0:1])
        p.op("vector", "tensor_copy", b1ac[0:w, i:i + 1], p.pb[6][0:w, 0:1])
        p.op("tensor", "transpose", p.pb[7][0:w, 0:1],
             b1row[0:1, F2 + 128 * i: F2 + 128 * i + w], cm.identf[0:1, 0:1])
        p.op("vector", "tensor_copy", b1gc[0:w, i:i + 1], p.pb[7][0:w, 0:1])
    gcol = col_layout2(p, cm, dd["lng"][ll:ll + 1, :], ND)
    bcol = col_layout2(p, cm, dd["lnb"][ll:ll + 1, :], ND)
    b2col = col_layout2(p, cm, dd["b2"][ll:ll + 1, :], ND)
    p.release(m_right, "right")

    xbuf = [[p.sb([128, CHF + 1], F32) for _ in range(ND)] for _ in range(2)]
    xn_b = [[p.sb([128, CHF], BF16) for _ in range(ND)] for _ in range(2)]
    y_b = [p.sb([128, CHF], BF16) for _ in range(2)]
    g2_b = [p.sb([128, CHF], BF16) for _ in range(2)]
    tmp = [p.sb([128, CHF], F32) for _ in range(2)]
    ln = [FeatLN(p, cm, CHF) for _ in range(2)]
    NCF = T // CHF

    def emit_front(cc):
        par = cc % 2
        t0 = cc * CHF
        for j in range(ND):
            if cc == 0:
                p.op("vector", "memset", xbuf[par][j][:, 0:1], 0.0)
                p.dma("sync", xbuf[par][j][:, 1:CHF + 1], src_dr[j * 128:(j + 1) * 128, 0:CHF])
            else:
                p.dma("sync", xbuf[par][j][:], src_dr[j * 128:(j + 1) * 128, t0 - 1:t0 + CHF])

        def sview(din):
            return (xbuf[par][din][:, 1:CHF + 1] if din < 4
                    else xbuf[par][din][:, 0:CHF])

        ln[par].stats([sview(j) for j in range(ND)], 6, 7)
        ln[par].rstd_plain(2)
        ln[par].bcast(6, 7)
        for dt in range(ND):
            p.op("vector", "scalar_tensor_tensor", tmp[par][:], sview(dt), 1.0,
                 p.pb[6][:, 0:CHF], ALU.bypass, ALU.subtract)
            p.op("vector", "tensor_mul", tmp[par][:], tmp[par][:], p.pb[7][:, 0:CHF])
            p.op("vector", "tensor_scalar", xn_b[par][dt][:], tmp[par][:],
                 gcol[:, dt:dt + 1], bcol[:, dt:dt + 1], ALU.mult, ALU.add)

    def emit_body(cc):
        par = cc % 2
        t0 = cc * CHF

        def d_w1(i):
            w = f2w(i)
            for din in range(ND):
                p.op("tensor", "matmul", p.pb[4][0:w, 0:CHF],
                     w1b[din][:, 128 * i:128 * i + w], xn_b[par][din][:],
                     start=(din == 0), stop=(din == ND - 1), skip_group_check=True)
                p.op("tensor", "matmul", p.pb[5][0:w, 0:CHF],
                     w1b[din][:, F2 + 128 * i: F2 + 128 * i + w], xn_b[par][din][:],
                     start=(din == 0), stop=(din == ND - 1), skip_group_check=True)
            p.op("scalar", "activation", out=g2_b[i % 2][0:w, :], in_=p.pb[5][0:w, 0:CHF],
                 func=AF.Gelu, bias=b1gc[0:w, i:i + 1])
            p.op("vector", "scalar_tensor_tensor", y_b[i % 2][0:w, :],
                 p.pb[4][0:w, 0:CHF], b1ac[0:w, i:i + 1], g2_b[i % 2][0:w, :],
                 ALU.add, ALU.mult)

        def d_w2(i):
            w = f2w(i)
            for dout in range(ND):
                bz = dout // 2
                zsl = slice((dout % 2) * CHF, (dout % 2) * CHF + CHF)
                p.op("tensor", "matmul", p.pb[bz][:, zsl],
                     w2b[i][0:w, dout * 128:(dout + 1) * 128], y_b[i % 2][0:w, :],
                     start=(i == 0 and dout % 2 == 0), stop=(i == NF2 - 1),
                     skip_group_check=True)
        d_w1(0)
        for i in range(1, NF2):
            d_w1(i)
            d_w2(i - 1)
        d_w2(NF2 - 1)
        for dout in range(ND):
            bz = dout // 2
            zsl = slice((dout % 2) * CHF, (dout % 2) * CHF + CHF)
            p.op("vector", "scalar_tensor_tensor", tmp[par][:], p.pb[bz][:, zsl],
                 b2col[:, dout:dout + 1], xbuf[par][dout][:, 1:CHF + 1], ALU.add, ALU.add)
            p.op("vector", "tensor_copy", xbuf[par][dout][:, 1:CHF + 1], tmp[par][:])
            p.dma("gpsimd", dst_dr[dout * 128:(dout + 1) * 128, t0:t0 + CHF],
                  tmp[par][:])

    emit_front(0)
    for cc in range(1, NCF):
        emit_front(cc)
        emit_body(cc - 1)
    emit_body(NCF - 1)
    p.release(m_w)


def host_l1f_consts():
    t = np.arange(T, dtype=np.float64)
    half = np.arange(0, 64, 2, dtype=np.float64)
    inv_freq = 1.0 / 10000.0 ** (half / 64.0)
    freqs = t[:, None] * inv_freq[None, :]
    base = (half + 0.4 * 64.0) / (1.4 * 64.0)
    power = (t - T // 2) / (256 // 2)
    scale = base[None, :] ** power[:, None]
    cos = np.cos(freqs); sin = np.sin(freqs)
    idx = np.arange(64) % 32
    sign = np.where(np.arange(64) < 32, -1.0, 1.0)
    cq = (cos * scale)[:, idx]
    sq = (sin * scale)[:, idx] * sign[None, :]
    ck = (cos / scale)[:, idx]
    sk = (sin / scale)[:, idx] * sign[None, :]
    pidx = np.arange(128) % 64
    out = {}
    for nm, tab in (("tab_cq", cq), ("tab_sq", sq), ("tab_ck", ck), ("tab_sk", sk)):
        out[nm] = np.ascontiguousarray(tab[:, pidx].T.astype(np.float32))
    i = np.arange(256)[None, :]
    pp = np.arange(128)[:, None]
    allow = [i <= pp, i <= pp + 128, i >= pp, i >= pp + 128]
    msk = np.concatenate([np.where(a, 0.0, -1e7) for a in allow], axis=0)
    out["masks"] = msk.astype(BF)
    hb2 = np.zeros((128, 2), np.float32)
    hb2[0:64, 0] = 1.0
    hb2[64:128, 1] = 1.0
    out["hb2b"] = hb2.astype(BF)
    out["hb2f"] = hb2
    out["hb2ft"] = np.ascontiguousarray(hb2.T)
    out["hb2ftb"] = np.ascontiguousarray(hb2.T).astype(BF)
    return out


def declare_l1f_inputs(nc):
    dd = {}
    dd["wqkv"] = nc.dram_tensor("l1f_wqkv", [6, D, 3072], F32, kind="ExternalInput")
    dd["qs"] = nc.dram_tensor("l1f_qs", [6, 64], F32, kind="ExternalInput")
    dd["ks"] = nc.dram_tensor("l1f_ks", [6, 64], F32, kind="ExternalInput")
    dd["wo"] = nc.dram_tensor("l1f_wo", [6, D, D], F32, kind="ExternalInput")
    dd["lng"] = nc.dram_tensor("l1f_lng", [6, D], F32, kind="ExternalInput")
    dd["lnb"] = nc.dram_tensor("l1f_lnb", [6, D], F32, kind="ExternalInput")
    dd["w1"] = nc.dram_tensor("l1f_w1", [6, D, W1F], F32, kind="ExternalInput")
    dd["b1"] = nc.dram_tensor("l1f_b1", [6, W1F], F32, kind="ExternalInput")
    dd["w2"] = nc.dram_tensor("l1f_w2", [6, F2, D], F32, kind="ExternalInput")
    dd["b2"] = nc.dram_tensor("l1f_b2", [6, D], F32, kind="ExternalInput")
    for nm, shp, dt in (("tab_cq", [128, T], F32), ("tab_sq", [128, T], F32),
                        ("tab_ck", [128, T], F32), ("tab_sk", [128, T], F32),
                        ("masks", [512, 256], BF16), ("hb2b", [128, 2], BF16),
                        ("hb2f", [128, 2], F32), ("hb2ft", [2, 128], F32),
                        ("hb2ftb", [2, 128], BF16)):
        dd[nm] = nc.dram_tensor(nm, shp, dt, kind="ExternalInput")
    return dd


def build_stage_d_program(nlayers=6):
    nc = bass.Bass("TRN2", target_bir_lowering=False)
    dr = declare_const_inputs(nc)
    dd = declare_l1f_inputs(nc)
    x3T = nc.dram_tensor("x3T", [D, T], F32, kind="ExternalInput")
    y = nc.dram_tensor("y", [T, D], F32, kind="ExternalOutput")
    ping = nc.dram_tensor("pingD", [D, T], F32, kind="Internal")
    pong = nc.dram_tensor("pongD", [D, T], F32, kind="Internal")
    p = Prog(nc)
    cm = CommonSB(p, dr)
    stage_d(p, cm, dd, x3T, y, ping, pong, nlayers=nlayers)
    p.barrier()
    p.finalize()
    return nc


def build_stage_c_program():
    nc = bass.Bass("TRN2", target_bir_lowering=False)
    dr = declare_const_inputs(nc)
    x2T = nc.dram_tensor("x2T", [D, T], F32, kind="ExternalInput")
    xpw = nc.dram_tensor("l1a_xpw", [96, D], F32, kind="ExternalInput")
    dtw = nc.dram_tensor("l1a_dtw", [D, 64], F32, kind="ExternalInput")
    dtb = nc.dram_tensor("l1a_dtb", [D], F32, kind="ExternalInput")
    alog = nc.dram_tensor("l1a_alog", [D, 16], F32, kind="ExternalInput")
    dpar = nc.dram_tensor("l1a_dpar", [D], F32, kind="ExternalInput")
    lna = nc.dram_tensor("l1a_lna", [D], F32, kind="ExternalInput")
    lnb = nc.dram_tensor("l1a_lnb", [D], F32, kind="ExternalInput")
    selb = nc.dram_tensor("selb", [2048, 128], BF16, kind="ExternalInput")
    pf = nc.dram_tensor("pf", [2048, 128], F32, kind="ExternalInput")
    rbt = nc.dram_tensor("rb", [16, 128], BF16, kind="ExternalInput")
    x3T = nc.dram_tensor("x3T", [D, T], F32, kind="ExternalOutput")
    p = Prog(nc)
    cm = CommonSB(p, dr)
    stage_c(p, cm, x2T, xpw, dtw, dtb, alog, dpar, lna, lnb, selb, pf, rbt, x3T)
    p.barrier()
    p.finalize()
    return nc


def build_stage_b_program():
    nc = bass.Bass("TRN2", target_bir_lowering=False)
    dr = declare_const_inputs(nc)
    x1T = nc.dram_tensor("x1T", [D, T], F32, kind="ExternalInput")
    gam = nc.dram_tensor("l0f_g", [D], F32, kind="ExternalInput")
    bet = nc.dram_tensor("l0f_b", [D], F32, kind="ExternalInput")
    q1 = nc.dram_tensor("l0f_q1", [D, 4096], I32, kind="ExternalInput")
    s1 = nc.dram_tensor("l0f_s1", [8, 4096], F32, kind="ExternalInput")
    b1 = nc.dram_tensor("l0f_b1", [1, 4096], F32, kind="ExternalInput")
    q2 = nc.dram_tensor("l0f_q2", [4096, D], I32, kind="ExternalInput")
    s2 = nc.dram_tensor("l0f_s2", [32, D], F32, kind="ExternalInput")
    b2 = nc.dram_tensor("l0f_b2", [D], F32, kind="ExternalInput")
    x2T = nc.dram_tensor("x2T", [D, T], F32, kind="ExternalOutput")
    p = Prog(nc)
    cm = CommonSB(p, dr)
    stage_b(p, cm, x1T, gam, bet, q1, s1, b1, q2, s2, b2, x2T)
    p.barrier()
    p.finalize()
    return nc


def build_full_program():
    nc = bass.Bass("TRN2", target_bir_lowering=False)
    dr = declare_const_inputs(nc)
    x = nc.dram_tensor("xa", [T, D], F32, kind="ExternalInput")
    ls = nc.dram_tensor("l0_ls", [1, D], F32, kind="ExternalInput")
    lb = nc.dram_tensor("l0_lb", [1, D], F32, kind="ExternalInput")
    wq = nc.dram_tensor("l0_wq", [D, D], F32, kind="ExternalInput")
    wk = nc.dram_tensor("l0_wk", [D, D], F32, kind="ExternalInput")
    wv = nc.dram_tensor("l0_wv", [D, D], F32, kind="ExternalInput")
    gam = nc.dram_tensor("l0f_g", [D], F32, kind="ExternalInput")
    bet = nc.dram_tensor("l0f_b", [D], F32, kind="ExternalInput")
    q1 = nc.dram_tensor("l0f_q1", [D, 4096], I32, kind="ExternalInput")
    s1 = nc.dram_tensor("l0f_s1", [8, 4096], F32, kind="ExternalInput")
    b1 = nc.dram_tensor("l0f_b1", [1, 4096], F32, kind="ExternalInput")
    q2 = nc.dram_tensor("l0f_q2", [4096, D], I32, kind="ExternalInput")
    s2 = nc.dram_tensor("l0f_s2", [32, D], F32, kind="ExternalInput")
    b2 = nc.dram_tensor("l0f_b2", [D], F32, kind="ExternalInput")
    xpw = nc.dram_tensor("l1a_xpw", [96, D], F32, kind="ExternalInput")
    dtw = nc.dram_tensor("l1a_dtw", [D, 64], F32, kind="ExternalInput")
    dtb = nc.dram_tensor("l1a_dtb", [D], F32, kind="ExternalInput")
    alog = nc.dram_tensor("l1a_alog", [D, 16], F32, kind="ExternalInput")
    dpar = nc.dram_tensor("l1a_dpar", [D], F32, kind="ExternalInput")
    lna = nc.dram_tensor("l1a_lna", [D], F32, kind="ExternalInput")
    lnb = nc.dram_tensor("l1a_lnb", [D], F32, kind="ExternalInput")
    selb = nc.dram_tensor("selb", [2048, 128], BF16, kind="ExternalInput")
    pf = nc.dram_tensor("pf", [2048, 128], F32, kind="ExternalInput")
    rbt = nc.dram_tensor("rb", [16, 128], BF16, kind="ExternalInput")
    dd = declare_l1f_inputs(nc)
    y = nc.dram_tensor("y", [T, D], F32, kind="ExternalOutput")
    x1T = nc.dram_tensor("x1T", [D, T], F32, kind="Internal")
    x2T = nc.dram_tensor("x2T", [D, T], F32, kind="Internal")
    x3T = nc.dram_tensor("x3T", [D, T], F32, kind="Internal")
    scrA = nc.dram_tensor("scrA", [514, T], F32, kind="Internal")
    ping = nc.dram_tensor("pingD", [D, T], F32, kind="Internal")
    pong = nc.dram_tensor("pongD", [D, T], F32, kind="Internal")

    p = Prog(nc)
    cm = CommonSB(p, dr)
    stage_a(p, cm, x, ls, lb, wq, wk, wv, x1T, scrA)
    stage_b(p, cm, x1T, gam, bet, q1, s1, b1, q2, s2, b2, x2T)
    stage_c(p, cm, x2T, xpw, dtw, dtb, alog, dpar, lna, lnb, selb, pf, rbt, x3T)
    stage_d(p, cm, dd, x3T, y, ping, pong, nlayers=6)
    p.barrier()
    p.finalize()
    return nc


def kernel(**inputs):
    cons = host_consts()
    cons.update(host_ssm_consts())
    cons.update(host_l1f_consts())
    nc = build_full_program()
    in_maps = []
    for b in range(B):
        m = dict(cons)
        m["xa"] = np.ascontiguousarray(inputs["L0_attn_x"][b])
        m["l0_ls"] = np.asarray(inputs["L0_attn_ln_scale"]).reshape(1, D)
        m["l0_lb"] = np.asarray(inputs["L0_attn_ln_bias"]).reshape(1, D)
        m["l0_wq"] = inputs["L0_attn_wq"]
        m["l0_wk"] = inputs["L0_attn_wk"]
        m["l0_wv"] = inputs["L0_attn_wv"]
        m["l0f_g"] = inputs["L0_ffn_ln_gamma"]
        m["l0f_b"] = inputs["L0_ffn_ln_beta"]
        m["l0f_q1"] = inputs["L0_ffn_fc1_q"]
        m["l0f_s1"] = inputs["L0_ffn_fc1_scales"]
        m["l0f_b1"] = np.asarray(inputs["L0_ffn_fc1_bias"]).reshape(1, -1)
        m["l0f_q2"] = inputs["L0_ffn_fc2_q"]
        m["l0f_s2"] = inputs["L0_ffn_fc2_scales"]
        m["l0f_b2"] = inputs["L0_ffn_fc2_bias"]
        m["l1a_xpw"] = inputs["L1_attn_x_proj_w"]
        m["l1a_dtw"] = inputs["L1_attn_dt_proj_w"]
        m["l1a_dtb"] = inputs["L1_attn_dt_proj_b"]
        m["l1a_alog"] = inputs["L1_attn_A_log"]
        m["l1a_dpar"] = inputs["L1_attn_D_param"]
        m["l1a_lna"] = inputs["L1_attn_ln_a"]
        m["l1a_lnb"] = inputs["L1_attn_ln_b"]
        m["l1f_wqkv"] = inputs["L1_ffn_Wqkv"]
        m["l1f_qs"] = inputs["L1_ffn_q_scale"]
        m["l1f_ks"] = inputs["L1_ffn_k_scale"]
        m["l1f_wo"] = inputs["L1_ffn_Wo"]
        m["l1f_lng"] = inputs["L1_ffn_ln_g"]
        m["l1f_lnb"] = inputs["L1_ffn_ln_b"]
        m["l1f_w1"] = inputs["L1_ffn_W1"]
        m["l1f_b1"] = inputs["L1_ffn_b1"]
        m["l1f_w2"] = inputs["L1_ffn_W2"]
        m["l1f_b2"] = inputs["L1_ffn_b2"]
        in_maps.append(m)
    res = run_bass_kernel_spmd(nc, in_maps, core_ids=list(range(B)))
    return np.stack([r["y"] for r in res.results])


# version 14 (final)
# speedup vs baseline: 1.0298x; 1.0298x over previous
import numpy as np
import ml_dtypes
import concourse.bass as bass
import concourse.mybir as mybir
from concourse.bass_utils import run_bass_kernel_spmd

F32 = mybir.dt.float32
BF16 = mybir.dt.bfloat16
I32 = mybir.dt.int32
AF = mybir.ActivationFunctionType
ALU = mybir.AluOpType
AX = mybir.AxisListType
BF = ml_dtypes.bfloat16

B, T, D = 8, 4096, 1024
NT, ND = T // 128, D // 128
H, DH = 16, 64
CH = 512
NCH = T // CH

CONST_VALS = [
    -12.0,
    1.0 / 32.0,
    1e-5,
    1e-6,
    -0.5,
    1.0,
    0.0,
    1.0 / 1024.0,
    -1.0,
    0.5,
    1024.0 / 1023.0,
    1e-12,
    8.0,
    -128.0,
    2.0794415416798357,
]
NCONST = 16
SBUF_LO = 16512
SBUF_HI = 229312

ENGS = ("sync", "scalar", "tensor", "vector", "gpsimd")


class Sem:
    def __init__(self, nc, name):
        self.h = nc.alloc_semaphore(name)
        self.n = 0


def _ap_name(a):
    try:
        return a.tensor.name
    except AttributeError:
        return None


_IO = {
    "tensor_copy": ((0,), (1,)),
    "tensor_tensor": ((0,), (1, 2)),
    "tensor_add": ((0,), (1, 2)),
    "tensor_sub": ((0,), (1, 2)),
    "tensor_mul": ((0,), (1, 2)),
    "tensor_max": ((0,), (1, 2)),
    "tensor_scalar": ((0,), (1, 2, 3)),
    "tensor_single_scalar": ((0,), (1, 2)),
    "tensor_scalar_min": ((0,), (1, 2)),
    "tensor_scalar_max": ((0,), (1, 2)),
    "tensor_scalar_add": ((0,), (1, 2)),
    "tensor_scalar_mul": ((0,), (1, 2)),
    "scalar_tensor_tensor": ((0,), (1, 2, 3)),
    "tensor_tensor_scan": ((0,), (1, 2, 3)),
    "tensor_reduce": ((0,), (1,)),
    "reduce_sum": ((0,), (1,)),
    "reduce_max": ((0,), (1,)),
    "memset": ((0,), ()),
    "memzero": ((0,), ()),
    "reciprocal": ((0,), (1,)),
    "select": ((0,), (1, 2, 3)),
    "copy_predicated": ((0,), (1, 2)),
    "stream_shuffle": ((0,), (1,)),
    "iota": ((0,), ()),
    "transpose": ((0,), (1, 2)),
    "matmul": ((0,), (1, 2)),
    "activation": (("out", "accum_out"), ("in_", "bias", "scale")),
    "dma_start": (("out",), ("in_",)),
}

ALL_CHAINS = ("scalar", "vector", "tensor", "gpsimd")


class Prog:

    def __init__(self, nc):
        self.nc = nc
        self.chain = {e: Sem(nc, "c_" + e) for e in ALL_CHAINS}
        self.pool = {}
        for q, n in (("sync", 10), ("scalar", 6), ("gpsimd", 6)):
            self.pool[q] = [Sem(nc, f"dma_{q}{i}") for i in range(n)]
        self._rr = {q: 0 for q in self.pool}
        self.ops = []
        self.per_eng_ops = {e: [] for e in ENGS}
        self.last_w = {}
        self.rsince = {}
        self.live = {}
        self.dead = {}
        self.alias_init = {}
        self._fence = None
        self._eng_last = {}
        self._cost = 500
        self._bytes = 0
        self.lo = SBUF_LO
        self.hi = SBUF_HI
        self._nm = 0
        self.pb = [nc.alloc_psum_tensor(f"pb{i}", [128, 512], F32) for i in range(8)]
        self.pbh = [t[:].bitcast(BF16) for t in self.pb]
        self.dbg = {}

    def sb(self, shape, dt=F32, side="left", name=None):
        nbytes = int(np.prod(shape[1:])) * mybir.dt.size(dt)
        nbytes = (nbytes + 31) & ~31
        self._nm += 1
        if side == "left":
            off = self.lo
            self.lo += nbytes
        else:
            self.hi -= nbytes
            off = self.hi
        assert self.lo <= self.hi, f"SBUF overflow: lo={self.lo} hi={self.hi}"
        nm = name or f"sb{self._nm}"
        span = (off, off + nbytes)
        h = self.nc.alloc_sbuf_tensor_at(nm, list(shape), dt, offset=off)
        real = h.name
        init = set()
        for dn, (dlo, dhi, deps) in list(self.dead.items()):
            if dlo < span[1] and span[0] < dhi:
                init.update(deps)
                if dlo >= span[0] and dhi <= span[1]:
                    del self.dead[dn]
        if init:
            self.alias_init[real] = init
        self.live[real] = span
        return h

    def mark(self, side="left"):
        return self.lo if side == "left" else self.hi

    def release(self, mark, side="left"):
        if side == "left":
            span = (mark, self.lo)
            self.lo = mark
        else:
            span = (self.hi, mark)
            self.hi = mark
        for nm, (tlo, thi, *_r) in list(self.live.items()):
            if tlo >= span[0] and thi <= span[1]:
                deps = set(self.rsince.get(nm, ()))
                w = self.last_w.get(nm)
                if w is not None:
                    deps.add(w)
                prior = self.alias_init.pop(nm, None)
                if prior:
                    deps.update(prior)
                if len(deps) > 48:
                    deps = {self._join(deps)}
                self.dead[nm + f"@{len(self.dead)}"] = (tlo, thi, deps)
                del self.live[nm]
                self.last_w.pop(nm, None)
                self.rsince.pop(nm, None)

    def _read(self, deps, name):
        w = self.last_w.get(name)
        if w is not None:
            deps.add(w)
        ai = self.alias_init.get(name)
        if ai:
            deps.update(ai)

    def _note_read(self, name, opid):
        self.rsince.setdefault(name, []).append(opid)

    def _write(self, deps, name):
        w = self.last_w.get(name)
        if w is not None:
            deps.add(w)
        deps.update(self.rsince.get(name, ()))
        ai = self.alias_init.get(name)
        if ai:
            deps.update(ai)

    def _note_write(self, name, opid):
        self.last_w[name] = opid
        self.rsince[name] = []
        self.alias_init.pop(name, None)

    def _join(self, deps):
        opid = len(self.ops)
        rec = {"id": opid, "eng": "gpsimd",
               "fn": lambda e: e.engine_nop(),
               "deps": set(deps), "odeps": set(), "key": ("eng", "gpsimd"),
               "inc": 1, "cost": 60, "bytes": 0}
        self.ops.append(rec)
        self._eng_last["gpsimd"] = opid
        return opid

    def _emit(self, eng, fn, reads, writes, prod_key):
        opid = len(self.ops)
        deps = set()
        for nm in reads:
            self._read(deps, nm)
        for nm in writes:
            self._write(deps, nm)
        odeps = set()
        if eng == "tensor":
            te = {d for d in deps if self.ops[d]["eng"] == "tensor"}
            deps -= te
            odeps |= te
        if self._fence is not None:
            odeps.add(self._fence)
        rec = {"id": opid, "eng": eng, "fn": fn, "deps": deps, "odeps": odeps,
               "key": prod_key, "inc": 16 if prod_key[0] == "dma" else 1,
               "cost": self._cost, "bytes": self._bytes}
        self.ops.append(rec)
        self._eng_last[eng] = opid
        for nm in reads:
            self._note_read(nm, opid)
        for nm in writes:
            self._note_write(nm, opid)
        return opid

    def op(self, eng, meth, *args, **kw):
        wspec, rspec = _IO[meth]
        reads, writes = [], []
        def grab(spec, out):
            for s in spec:
                a = kw.get(s) if isinstance(s, str) else (args[s] if s < len(args) else None)
                nm = _ap_name(a) if a is not None else None
                if nm is not None:
                    out.append(nm)
        grab(wspec, writes)
        grab(rspec, reads)
        try:
            a0 = kw.get("out") if "out" in kw else args[0]
            F = int(a0.free_size()) if hasattr(a0, "free_size") else 512
        except Exception:
            F = 512
        if eng == "tensor":
            c = max(90, int(F * 0.45) + 50)
        elif eng == "scalar":
            c = int(190 + F / 1.2)
        elif eng == "gpsimd":
            c = int(500 + F / 1.2)
        else:
            c = int(160 + F / 0.96)
        self._cost = c
        self._bytes = 0

        def fn(e, meth=meth, args=args, kw=kw):
            return getattr(e, meth)(*args, **kw)
        return self._emit(eng, fn, reads, writes, ("eng", eng))

    def dma(self, q, out, in_):
        pool = self.pool[q]
        sem = pool[self._rr[q]]
        self._rr[q] = (self._rr[q] + 1) % len(pool)
        slot = "slot:" + sem.h.name if hasattr(sem.h, "name") else "slot:" + str(id(sem))

        def fn(e, out=out, in_=in_):
            return e.dma_start(out=out, in_=in_)
        nm_o, nm_i = _ap_name(out), _ap_name(in_)
        try:
            self._bytes = int(out.nbytes() if hasattr(out, "nbytes") else 0)
        except Exception:
            self._bytes = 65536
        self._cost = 300
        return self._emit(q, fn, [nm_i, slot], [nm_o, slot], ("dma", id(sem), sem))

    def barrier(self):
        opid = len(self.ops)
        rec = {"id": opid, "eng": "BARRIER", "fn": None, "deps": set(),
               "odeps": set(self._eng_last.values()), "key": None, "inc": 0,
               "cost": 0, "bytes": 0}
        self.ops.append(rec)
        self.last_w.clear()
        self.rsince.clear()
        self._fence = opid

    def finalize(self):
        import heapq
        ops = self.ops
        n = len(ops)
        preds = [None] * n
        succs = [[] for _ in range(n)]
        indeg = [0] * n
        real = []
        for o in ops:
            i = o["id"]
            if o["eng"] == "BARRIER":
                preds[i] = set()
                continue
            ps = set(o["deps"]) | set(o["odeps"])
            ps = {p for p in ps if ops[p]["eng"] != "BARRIER"}
            preds[i] = ps
            for p_ in ps:
                succs[p_].append(i)
            indeg[i] = len(ps)
            real.append(i)
        prio = [0.0] * n
        for i in reversed(range(n)):
            if ops[i]["eng"] == "BARRIER":
                continue
            m = 0.0
            for s_ in succs[i]:
                if prio[s_] > m:
                    m = prio[s_]
            prio[i] = ops[i]["cost"] + m
        SEM_LAT = 150.0
        heaps = {e: [] for e in ENGS}
        t_eng = {e: 0.0 for e in ENGS}
        comp = [0.0] * n
        for i in real:
            if indeg[i] == 0:
                heapq.heappush(heaps[ops[i]["eng"]], (-prio[i], i))
        sched = {e: [] for e in ENGS}
        remaining = len(real)
        while remaining:
            best_e, best_start = None, None
            for e in ENGS:
                if not heaps[e]:
                    continue
                i = heaps[e][0][1]
                st = t_eng[e]
                for p_ in preds[i]:
                    if comp[p_] > st:
                        st = comp[p_]
                if best_start is None or st < best_start:
                    best_e, best_start = e, st
            e = best_e
            _, i = heapq.heappop(heaps[e])
            st = t_eng[e]
            for p_ in preds[i]:
                if comp[p_] > st:
                    st = comp[p_]
            o = ops[i]
            fin = st + o["cost"]
            t_eng[e] = fin
            lat = SEM_LAT
            if o["key"][0] == "dma":
                lat = 1300.0 + o["bytes"] / 358.0
            comp[i] = fin + lat
            sched[e].append(i)
            remaining -= 1
            for s_ in succs[i]:
                indeg[s_] -= 1
                if indeg[s_] == 0:
                    heapq.heappush(heaps[ops[s_]["eng"]], (-prio[s_], s_))
        self.makespan_ns = max(t_eng.values())
        need = [False] * n
        for o in ops:
            if o["eng"] == "BARRIER":
                continue
            for pid in o["deps"]:
                need[pid] = True
        for e in ENGS:
            if sched[e]:
                need[sched[e][-1]] = True
        val = [0] * n
        counts = {}
        for e in ENGS:
            for i in sched[e]:
                o = ops[i]
                if not need[i]:
                    continue
                k = o["key"]
                counts[k] = counts.get(k, 0) + o["inc"]
                val[i] = counts[k]

        def sem_of_key(k):
            return k[2].h if k[0] == "dma" else self.chain[k[1]].h

        streams = {e: [] for e in ENGS}
        for e in ENGS:
            seen = {}
            for i in sched[e]:
                o = ops[i]
                byk = {}
                for pid in o["deps"]:
                    assert need[pid], f"dep on un-inc'd op {pid}"
                    kk = ops[pid]["key"]
                    v = val[pid]
                    if byk.get(kk, 0) < v:
                        byk[kk] = v
                for kk, v in sorted(byk.items(), key=lambda kv: kv[1]):
                    if seen.get(kk, 0) >= v:
                        continue
                    h = sem_of_key(kk)
                    streams[e].append(lambda eng_, h=h, v=v: eng_.wait_ge(h, v))
                    seen[kk] = v
                if need[i]:
                    h = sem_of_key(o["key"])
                    inc = o["inc"]
                    fn = o["fn"]
                    def fi(eng_, fn=fn, h=h, inc=inc):
                        fn(eng_).then_inc(h, inc)
                    streams[e].append(fi)
                else:
                    streams[e].append(o["fn"])
            todo = []
            for k, v in counts.items():
                if seen.get(k, 0) < v:
                    todo.append((sem_of_key(k), v))
            def fb(eng_, todo=todo):
                for h, v in todo:
                    eng_.wait_ge(h, v)
            streams[e].append(fb)
        with self.nc.Block() as blk:
            for e in ENGS:
                def body(eng_, fns=streams[e]):
                    for f in fns:
                        f(eng_)
                getattr(blk, e)(body)


def host_consts():
    cst = np.zeros((128, NCONST), np.float32)
    for i, v in enumerate(CONST_VALS):
        cst[:, i] = v
    return {
        "consts": cst,
        "identb": np.eye(128, dtype=BF),
        "identf": np.eye(128, dtype=np.float32),
        "ones_row_b": np.ones((1, 128), BF),
        "ones_row_f": np.ones((1, 128), np.float32),
        "ones_col_b": np.ones((128, 1), BF),
    }


def declare_const_inputs(nc):
    d = {}
    d["consts"] = nc.dram_tensor("consts", [128, NCONST], F32, kind="ExternalInput")
    d["identb"] = nc.dram_tensor("identb", [128, 128], BF16, kind="ExternalInput")
    d["identf"] = nc.dram_tensor("identf", [128, 128], F32, kind="ExternalInput")
    d["ones_row_b"] = nc.dram_tensor("ones_row_b", [1, 128], BF16, kind="ExternalInput")
    d["ones_row_f"] = nc.dram_tensor("ones_row_f", [1, 128], F32, kind="ExternalInput")
    d["ones_col_b"] = nc.dram_tensor("ones_col_b", [128, 1], BF16, kind="ExternalInput")
    return d


class CommonSB:

    def __init__(self, p: Prog, dr):
        self.cst = p.sb([128, NCONST], F32)
        self.identb = p.sb([128, 128], BF16)
        self.identf = p.sb([128, 128], F32)
        self.ones_row_b = p.sb([1, 128], BF16)
        self.ones_row_f = p.sb([1, 128], F32)
        self.ones_col_b = p.sb([128, 1], BF16)
        self.ready = []
        for t, k in [(self.cst, "consts"), (self.identb, "identb"), (self.identf, "identf"),
                     (self.ones_row_b, "ones_row_b"), (self.ones_row_f, "ones_row_f"),
                     (self.ones_col_b, "ones_col_b")]:
            self.ready.append(p.dma("sync", t[:], dr[k][:]))

    def c(self, i):
        return self.cst[:, i:i + 1]

    def c1(self, i):
        return self.cst[0:1, i:i + 1]



def tok_ln_pair(p, cm, xb, sq, xn0, stat, eps1_col, eps2_col):
    p.op("vector", "reduce_sum", stat[:, 0:1], xb[:], axis=AX.X)
    p.op("vector", "tensor_scalar", stat[:, 1:2], stat[:, 0:1], 1.0 / 1024.0, None, ALU.mult)
    p.op("scalar", "activation", out=sq[:], in_=xb[:], func=AF.Square)
    p.op("vector", "reduce_sum", stat[:, 2:3], sq[:], axis=AX.X)
    p.op("vector", "tensor_mul", stat[:, 3:4], stat[:, 1:2], stat[:, 1:2])
    p.op("vector", "scalar_tensor_tensor", stat[:, 4:5], stat[:, 2:3], 1.0 / 1024.0,
         stat[:, 3:4], ALU.mult, ALU.subtract)
    p.op("scalar", "activation", out=stat[:, 5:6], in_=stat[:, 4:5], func=AF.Ln, bias=eps1_col)
    p.op("scalar", "activation", out=stat[:, 5:6], in_=stat[:, 5:6], func=AF.Exp, scale=cm.c(4))
    p.op("vector", "tensor_scalar", xn0[:], xb[:], stat[:, 1:2], stat[:, 5:6],
         ALU.subtract, ALU.mult)
    p.op("vector", "reduce_sum", stat[:, 6:7], xn0[:], axis=AX.X)
    p.op("vector", "tensor_scalar", stat[:, 7:8], stat[:, 6:7], 1.0 / 1024.0, None, ALU.mult)
    p.op("scalar", "activation", out=sq[:], in_=xn0[:], func=AF.Square)
    p.op("vector", "reduce_sum", stat[:, 8:9], sq[:], axis=AX.X)
    p.op("vector", "tensor_mul", stat[:, 9:10], stat[:, 7:8], stat[:, 7:8])
    p.op("vector", "scalar_tensor_tensor", stat[:, 10:11], stat[:, 8:9], 1.0 / 1024.0,
         stat[:, 9:10], ALU.mult, ALU.subtract)
    p.op("scalar", "activation", out=stat[:, 11:12], in_=stat[:, 10:11], func=AF.Ln, bias=eps2_col)
    p.op("scalar", "activation", out=stat[:, 11:12], in_=stat[:, 11:12], func=AF.Exp, scale=cm.c(4))


def stage_a(p: Prog, cm: CommonSB, x_dr, ls_dr, lb_dr, wq_dr, wk_dr, wv_dr,
            x1T_dr, scr_dr):
    mark0 = p.mark()
    xinT = [p.sb([128, T], BF16, name=f"xinT{j}") for j in range(ND)]
    p.dbg["xinT"] = xinT

    m_setup = p.mark()
    lsr = p.sb([1, D], F32)
    lbr = p.sb([1, D], F32)
    srep = p.sb([128, D], F32)
    brep = p.sb([128, D], F32)
    p.dma("sync", lsr[:], ls_dr[:])
    p.dma("sync", lbr[:], lb_dr[:])
    for half in range(2):
        sli = slice(half * 512, half * 512 + 512)
        p.op("tensor", "matmul", p.pb[0][:, :], cm.ones_row_f[:], lsr[:, sli],
             start=True, stop=True)
        p.op("tensor", "matmul", p.pb[1][:, :], cm.ones_row_f[:], lbr[:, sli],
             start=True, stop=True)
        p.op("vector", "tensor_copy", srep[:, sli], p.pb[0][:, :])
        p.op("vector", "tensor_copy", brep[:, sli], p.pb[1][:, :])

    xsb = [p.sb([128, D], F32) for _ in range(2)]
    sq = p.sb([128, D], F32)
    xn0 = p.sb([128, D], F32)
    tmp = p.sb([128, D], F32)
    xin_b = [p.sb([128, D], BF16) for _ in range(2)]
    stat = p.sb([128, 16], F32)

    for i in range(NT):
        xb = xsb[i % 2]
        p.dma("sync", xb[:], x_dr[i * 128:(i + 1) * 128, :])
        tok_ln_pair(p, cm, xb, sq, xn0, stat, cm.c(2), cm.c(3))
        p.op("vector", "tensor_scalar", tmp[:], xn0[:], stat[:, 7:8], stat[:, 11:12],
             ALU.subtract, ALU.mult)
        p.op("vector", "tensor_mul", tmp[:], tmp[:], srep[:])
        p.op("vector", "tensor_add", xin_b[i % 2][:], tmp[:], brep[:])
        for j in range(ND):
            p.op("tensor", "transpose", p.pbh[j % 2][:, 0:128],
                 xin_b[i % 2][:, j * 128:(j + 1) * 128], cm.identb[:])
            p.op("scalar", "activation", out=xinT[j][:, i * 128:(i + 1) * 128],
                 in_=p.pbh[j % 2][:, 0:128], func=AF.Copy)
    p.release(m_setup)

    m_half = p.mark()
    wst = [p.sb([128, 512], F32) for _ in range(2)]
    wkvb = [p.sb([128, 512], BF16) for _ in range(ND)]
    wqb = [p.sb([128, 512], BF16) for _ in range(ND)]
    Vh = [p.sb([128, 8 * 65], BF16) for _ in range(NT)]
    kTh = [p.sb([128, T], BF16) for _ in range(4)]
    qTh = [p.sb([128, CH], BF16) for _ in range(4)]
    qp = [p.sb([128, CH], BF16) for _ in range(2)]
    probs = [p.sb([128, CH], BF16) for _ in range(4)]
    sum_f = [p.sb([128, CH], F32) for _ in range(4)]
    cast_b = p.sb([128, CH], BF16)
    den_rec = p.sb([1, CH], F32)
    den_h = p.sb([1, CH], BF16)
    den_l = p.sb([1, CH], BF16)
    rrep = p.sb([64, CH], F32)
    rowA = p.sb([1, CH], F32)
    rowB = p.sb([1, CH], F32)
    rowC = den_rec
    sumh0_sb = [p.sb([128, CH], F32) for _ in range(4)]
    xsb2 = [p.sb([128, D], F32) for _ in range(1)]
    xTc = [p.sb([128, CH], F32) for _ in range(ND)]

    def load_weight_half(w_dr, half, dest):
        for din in range(ND):
            p.dma("scalar", wst[din % 2][:],
                  w_dr[din * 128:(din + 1) * 128, half * 512:half * 512 + 512])
            p.op("gpsimd", "tensor_copy", dest[din][:], wst[din % 2][:])

    for half in range(2):
        jt0 = 4 * half

        load_weight_half(wv_dr, half, wkvb)
        for tt in range(NT):
            bank = 2 + tt % 2
            for din in range(ND):
                p.op("tensor", "matmul", p.pb[bank][:, :],
                     xinT[din][:, tt * 128:(tt + 1) * 128], wkvb[din][:],
                     start=(din == 0), stop=(din == ND - 1))
            vv = Vh[tt][:].rearrange("p (h c) -> p h c", c=65)
            p.op("vector", "tensor_copy", vv[:, :, 0:64], p.pb[bank][:, :])
            p.op("vector", "memset", vv[:, :, 64:65], 1.0)

        load_weight_half(wk_dr, half, wkvb)
        it = 0
        for dt in range(4):
            for cc in range(NCH):
                bank = it % 2
                for din in range(ND):
                    p.op("tensor", "matmul", p.pb[bank][:, :],
                         wkvb[din][:, dt * 128:(dt + 1) * 128],
                         xinT[din][:, cc * CH:(cc + 1) * CH],
                         start=(din == 0), stop=(din == ND - 1))
                p.op("scalar", "activation", out=kTh[dt][:, cc * CH:(cc + 1) * CH],
                     in_=p.pb[bank][:, :], func=AF.Copy)
                it += 1

        load_weight_half(wq_dr, half, wqb)

        for cc in range(NCH):
            for dt in range(4):
                bank = 6 + dt % 2
                for din in range(ND):
                    p.op("tensor", "matmul", p.pb[bank][:, :],
                         wqb[din][:, dt * 128:(dt + 1) * 128],
                         xinT[din][:, cc * CH:(cc + 1) * CH],
                         start=(din == 0), stop=(din == ND - 1))
                p.op("scalar", "activation", out=qTh[dt][:],
                     in_=p.pb[bank][:, :], func=AF.Copy)
            if cc == 0 and half == 0:
                p.op("vector", "memset", qp[0][64:128, :], 0.0)
                p.op("vector", "memset", qp[1][0:64, :], 0.0)
            for hh in range(8):
                ht = hh // 2
                hb = 64 * (hh % 2)
                p.op("vector", "tensor_copy", qp[hh % 2][hb:hb + 64, :],
                     qTh[ht][hb:hb + 64, :])
                def a_qk(kj):
                    bank = kj % 4
                    p.op("tensor", "matmul", p.pb[bank][:, :],
                         kTh[ht][:, kj * 128:(kj + 1) * 128],
                         qp[hh % 2][:],
                         start=True, stop=True)
                    p.op("scalar", "activation", out=probs[kj % 4][:],
                         in_=p.pb[bank][:, :], func=AF.Exp,
                         scale=cm.c(1), bias=cm.c(0))

                def a_pv(kj):
                    vv = Vh[kj][:].rearrange("p (h c) -> p h c", c=65)[:, hh, :]
                    p.op("tensor", "matmul", p.pb[4][0:65, :], vv, probs[kj % 4][:],
                         start=(kj == 0), stop=(kj == NT - 1), skip_group_check=True)
                for kj in range(NT):
                    a_qk(kj)
                    if kj >= 2:
                        a_pv(kj - 2)
                a_pv(NT - 2)
                a_pv(NT - 1)
                p.op("scalar", "activation", out=den_rec[:], in_=p.pb[4][64:65, :], func=AF.Ln)
                p.op("scalar", "activation", out=den_rec[:], in_=den_rec[:], func=AF.Exp,
                     scale=cm.c1(8))
                p.op("scalar", "activation", out=den_h[:], in_=den_rec[:], func=AF.Copy)
                p.op("vector", "tensor_sub", den_l[:], den_rec[:], den_h[:])
                p.op("tensor", "matmul", p.pb[5][0:64, :], cm.ones_row_b[:, 0:64],
                     den_h[:], start=True, stop=False, skip_group_check=True)
                p.op("tensor", "matmul", p.pb[5][0:64, :], cm.ones_row_b[:, 0:64],
                     den_l[:], start=False, stop=True, skip_group_check=True)
                p.op("vector", "tensor_copy", rrep[:], p.pb[5][0:64, :])
                p.op("vector", "scalar_tensor_tensor",
                     sum_f[ht][hb:hb + 64, :], p.pb[4][0:64, :], 1.0, rrep[:],
                     ALU.mult, ALU.mult)
            for dt in range(4):
                j = jt0 + dt
                p.op("vector", "tensor_add", sum_f[dt][:], sum_f[dt][:],
                     xinT[j][:, cc * CH:(cc + 1) * CH])
                p.op("scalar", "activation", out=cast_b[:], in_=sum_f[dt][:], func=AF.Copy)
                p.op("tensor", "matmul", p.pb[0][0:1, :], cm.ones_col_b[:], cast_b[:],
                     start=(dt == 0), stop=(dt == 3), skip_group_check=True)
                p.op("scalar", "activation", out=cast_b[:], in_=sum_f[dt][:], func=AF.Square)
                p.op("tensor", "matmul", p.pb[1][0:1, :], cm.ones_col_b[:], cast_b[:],
                     start=(dt == 0), stop=(dt == 3), skip_group_check=True)
            if half == 0:
                p.op("vector", "tensor_copy", rowA[:], p.pb[0][0:1, :])
                p.op("vector", "tensor_copy", rowB[:], p.pb[1][0:1, :])
                for dt in range(4):
                    p.dma("gpsimd", scr_dr[dt * 128:(dt + 1) * 128, cc * CH:(cc + 1) * CH],
                          sum_f[dt][:])
                p.dma("gpsimd", scr_dr[512:513, cc * CH:(cc + 1) * CH], rowA[:])
                p.dma("gpsimd", scr_dr[513:514, cc * CH:(cc + 1) * CH], rowB[:])
            else:
                p.dma("sync", rowA[:], scr_dr[512:513, cc * CH:(cc + 1) * CH])
                p.dma("sync", rowB[:], scr_dr[513:514, cc * CH:(cc + 1) * CH])
                for dt in range(4):
                    p.dma("sync", sumh0_sb[dt][:],
                          scr_dr[dt * 128:(dt + 1) * 128, cc * CH:(cc + 1) * CH])
                for q4 in range(4):
                    p.dma("sync", xsb2[0][:],
                          x_dr[cc * CH + q4 * 128: cc * CH + (q4 + 1) * 128, :])
                    for j in range(ND):
                        gi = q4 * ND + j
                        bank = 6 + gi % 2
                        p.op("tensor", "transpose", p.pb[bank][:, 0:128],
                             xsb2[0][:, j * 128:(j + 1) * 128], cm.identf[:])
                        p.op("vector", "tensor_copy",
                             xTc[j][:, q4 * 128:(q4 + 1) * 128], p.pb[bank][:, 0:128])
                p.op("vector", "scalar_tensor_tensor", rowA[:], p.pb[0][0:1, :], 1.0,
                     rowA[:], ALU.bypass, ALU.add)
                p.op("vector", "scalar_tensor_tensor", rowB[:], p.pb[1][0:1, :], 1.0,
                     rowB[:], ALU.bypass, ALU.add)
                p.op("vector", "tensor_scalar", rowA[:], rowA[:], 1.0 / 1024.0, None, ALU.mult)
                p.op("vector", "tensor_scalar", rowB[:], rowB[:], 1.0 / 1024.0, None, ALU.mult)
                p.op("vector", "tensor_mul", rowC[:], rowA[:], rowA[:])
                p.op("vector", "tensor_sub", rowB[:], rowB[:], rowC[:])
                p.op("scalar", "activation", out=rowB[:], in_=rowB[:], func=AF.Ln, bias=cm.c1(2))
                p.op("scalar", "activation", out=rowB[:], in_=rowB[:], func=AF.Exp, scale=cm.c1(4))
                p.op("tensor", "matmul", p.pb[2][:, :], cm.ones_row_f[:], rowA[:],
                     start=True, stop=True)
                p.op("tensor", "matmul", p.pb[3][:, :], cm.ones_row_f[:], rowB[:],
                     start=True, stop=True)
                for j in range(ND):
                    src = sumh0_sb[j] if j < 4 else sum_f[j - 4]
                    p.op("vector", "scalar_tensor_tensor", src[:], src[:], 1.0,
                         p.pb[2][:, :], ALU.bypass, ALU.subtract)
                    p.op("vector", "tensor_mul", src[:], src[:], p.pb[3][:, :])
                    p.op("vector", "tensor_add", xTc[j][:], xTc[j][:], src[:])
                    p.dma("gpsimd", x1T_dr[j * 128:(j + 1) * 128, cc * CH:(cc + 1) * CH],
                          xTc[j][:])
    p.release(m_half)
    p.release(mark0)



def col_layout(p, cm, vec_dr, n, name=None):
    rows = p.sb([1, 128 * n], F32, side="right")
    cols = p.sb([128, n], F32, name=name)
    p.dma("sync", rows[:], vec_dr.rearrange("(a q) -> a q", q=128 * n) if n > 0 else vec_dr)
    for j in range(n):
        bank = j % 2
        p.op("tensor", "transpose", p.pb[bank][:, 0:1], rows[0:1, j * 128:(j + 1) * 128],
             cm.identf[0:1, 0:1])
        p.op("vector", "tensor_copy", cols[:, j:j + 1], p.pb[bank][:, 0:1])
    return cols


class FeatLN:

    def __init__(self, p, cm, W):
        self.p, self.cm, self.W = p, cm, W
        self.cast = p.sb([128, W], BF16)
        self.rowS = p.sb([1, W], F32)
        self.rowQ = p.sb([1, W], F32)
        self.rowT = p.sb([1, W], F32)

    def stats(self, tiles, b_sum, b_sq, sl=None):
        p, cm, W = self.p, self.cm, self.W
        nd = len(tiles)
        for dt, tl in enumerate(tiles):
            src = tl[:, sl] if sl is not None else tl[:]
            p.op("scalar", "activation", out=self.cast[:], in_=src, func=AF.Copy)
            p.op("tensor", "matmul", p.pb[b_sum][0:1, :W], cm.ones_col_b[:], self.cast[:],
                 start=(dt == 0), stop=(dt == nd - 1), skip_group_check=True)
            p.op("scalar", "activation", out=self.cast[:], in_=src, func=AF.Square)
            p.op("tensor", "matmul", p.pb[b_sq][0:1, :W], cm.ones_col_b[:], self.cast[:],
                 start=(dt == 0), stop=(dt == nd - 1), skip_group_check=True)
        p.op("vector", "tensor_scalar", self.rowS[:], p.pb[b_sum][0:1, :W],
             1.0 / 1024.0, None, ALU.mult)
        p.op("vector", "tensor_scalar", self.rowQ[:], p.pb[b_sq][0:1, :W],
             1.0 / 1024.0, None, ALU.mult)
        p.op("vector", "tensor_mul", self.rowT[:], self.rowS[:], self.rowS[:])
        p.op("vector", "tensor_sub", self.rowQ[:], self.rowQ[:], self.rowT[:])

    def rstd_plain(self, eps_i):
        p, cm = self.p, self.cm
        p.op("scalar", "activation", out=self.rowQ[:], in_=self.rowQ[:], func=AF.Ln,
             bias=cm.c1(eps_i))
        p.op("scalar", "activation", out=self.rowQ[:], in_=self.rowQ[:], func=AF.Exp,
             scale=cm.c1(4))

    def bcast(self, b_mu, b_rs):
        p, cm, W = self.p, self.cm, self.W
        p.op("tensor", "matmul", p.pb[b_mu][:, :W], cm.ones_row_f[:], self.rowS[:],
             start=True, stop=True)
        p.op("tensor", "matmul", p.pb[b_rs][:, :W], cm.ones_row_f[:], self.rowQ[:],
             start=True, stop=True)

    def apply_plain(self, tiles_in, tiles_out, b_mu, b_rs, tmp, sl=None, add_to=None):
        p = self.p
        for dt, (ti, to) in enumerate(zip(tiles_in, tiles_out)):
            src = ti[:, sl] if sl is not None else ti[:]
            p.op("vector", "scalar_tensor_tensor", tmp[:], src, 1.0,
                 p.pb[b_mu][:, :self.W], ALU.bypass, ALU.subtract)
            if add_to is None:
                p.op("vector", "tensor_mul", to, tmp[:], p.pb[b_rs][:, :self.W])
            else:
                p.op("vector", "tensor_mul", tmp[:], tmp[:], p.pb[b_rs][:, :self.W])
                p.op("vector", "tensor_add", to, tmp[:], add_to[dt][:])



def stage_b(p: Prog, cm: CommonSB, x1T_dr, gam_dr, bet_dr, q1_dr, s1_dr, b1_dr,
            q2_dr, s2_dr, b2_dr, x2T_dr):
    F1 = 4096
    CHB = 256
    NCHB = T // CHB
    mark0 = p.mark()
    w1b = [p.sb([128, F1], BF16, name=f"w1b{g}") for g in range(8)]
    w2b = [p.sb([128, D], BF16, name=f"w2b{g}") for g in range(32)]

    m_dq = p.mark()
    qst = [p.sb([128, F1], I32)]
    qf = p.sb([128, F1], F32)
    srow = p.sb([1, F1], F32)
    srep = p.sb([128, F1], F32)
    for g in range(8):
        p.dma("scalar", qst[0][:, 0:2048], q1_dr[g * 128:(g + 1) * 128, 0:2048])
        p.dma("scalar", qst[0][:, 2048:F1], q1_dr[g * 128:(g + 1) * 128, 2048:F1])
        p.dma("scalar", srow[:], s1_dr[g:g + 1, :])
        for h4 in range(8):
            sli = slice(h4 * 512, h4 * 512 + 512)
            p.op("tensor", "matmul", p.pb[h4 % 2][:, :], cm.ones_row_f[:], srow[:, sli],
                 start=True, stop=True)
            p.op("vector", "tensor_copy", srep[:, sli], p.pb[h4 % 2][:, :])
        p.op("gpsimd", "tensor_copy", qf[:], qst[0][:])
        p.op("vector", "scalar_tensor_tensor", w1b[g][:], qf[:], -128.0, srep[:],
             ALU.add, ALU.mult)
    for g in range(32):
        p.dma("scalar", qst[0][:, 0:D], q2_dr[g * 128:(g + 1) * 128, :])
        p.dma("scalar", srow[:, 0:D], s2_dr[g:g + 1, :])
        for h4 in range(2):
            sli = slice(h4 * 512, h4 * 512 + 512)
            p.op("tensor", "matmul", p.pb[h4 % 2][:, :], cm.ones_row_f[:], srow[:, sli],
                 start=True, stop=True)
            p.op("vector", "tensor_copy", srep[:, sli], p.pb[h4 % 2][:, :])
        p.op("gpsimd", "tensor_copy", qf[:, 0:D], qst[0][:, 0:D])
        p.op("vector", "scalar_tensor_tensor", w2b[g][:], qf[:, 0:D], -128.0, srep[:, 0:D],
             ALU.add, ALU.mult)
    p.release(m_dq)

    m_right = p.mark("right")
    gcol = col_layout(p, cm, gam_dr, ND)
    bcol = col_layout(p, cm, bet_dr, ND)
    b2col = col_layout(p, cm, b2_dr, ND)
    hb2col = p.sb([128, ND], F32)
    p.op("vector", "tensor_scalar", hb2col[:], b2col[:], 0.5, None, ALU.mult)
    b1row = p.sb([1, F1], F32, side="right")
    b1cols = p.sb([128, 32], F32)
    p.dma("sync", b1row[:], b1_dr[:])
    for j in range(32):
        p.op("tensor", "transpose", p.pb[j % 2][:, 0:1], b1row[:, j * 128:(j + 1) * 128],
             cm.identf[0:1, 0:1])
        p.op("vector", "tensor_copy", b1cols[:, j:j + 1], p.pb[j % 2][:, 0:1])
    p.release(m_right, "right")

    x1c = [p.sb([128, CHB], F32) for _ in range(ND)]
    xnc = [p.sb([128, CHB], F32) for _ in range(ND)]
    hb = [p.sb([128, CHB], BF16) for _ in range(ND)]
    yb = [p.sb([128, CHB], BF16) for _ in range(32)]
    tmp = p.sb([128, CHB], F32)
    tmp2 = tmp
    ln = FeatLN(p, cm, CHB)

    for cc in range(NCHB):
        tsl = slice(cc * CHB, (cc + 1) * CHB)
        for j in range(ND):
            p.dma("sync", x1c[j][:], x1T_dr[j * 128:(j + 1) * 128, tsl])
        ln.stats(x1c, 0, 1)
        ln.rstd_plain(2)
        ln.bcast(0, 1)
        ln.apply_plain(x1c, [t[:] for t in xnc], 0, 1, tmp)
        ln.stats(xnc, 2, 3)
        ln.rstd_plain(2)
        ln.bcast(2, 3)
        for dt in range(ND):
            p.op("vector", "scalar_tensor_tensor", tmp[:], xnc[dt][:], 1.0,
                 p.pb[2][:, :CHB], ALU.bypass, ALU.subtract)
            p.op("vector", "tensor_mul", tmp[:], tmp[:], p.pb[3][:, :CHB])
            p.op("vector", "tensor_scalar", hb[dt][:], tmp[:], gcol[:, dt:dt + 1],
                 bcol[:, dt:dt + 1], ALU.mult, ALU.add)
        for ft in range(32):
            bank = 4 + ft % 2
            for din in range(ND):
                p.op("tensor", "matmul", p.pb[bank][:, :CHB],
                     w1b[din][:, ft * 128:(ft + 1) * 128], hb[din][:],
                     start=(din == 0), stop=(din == ND - 1))
            p.op("scalar", "activation", out=yb[ft][:], in_=p.pb[bank][:, :CHB],
                 func=AF.Silu, bias=b1cols[:, ft:ft + 1])
        for dout in range(ND):
            bank = 6 + dout % 2
            for ft in range(32):
                p.op("tensor", "matmul", p.pb[bank][:, :CHB],
                     w2b[ft][:, dout * 128:(dout + 1) * 128], yb[ft][:],
                     start=(ft == 0), stop=(ft == 31))
            p.op("vector", "tensor_scalar", tmp2[:], p.pb[bank][:, :CHB], 0.5,
                 hb2col[:, dout:dout + 1], ALU.mult, ALU.add)
            p.op("vector", "tensor_add", xnc[dout][:], xnc[dout][:], tmp2[:])
        ln.stats(xnc, 0, 1)
        ln.rstd_plain(2)
        ln.bcast(0, 1)
        for dt in range(ND):
            p.op("vector", "scalar_tensor_tensor", tmp[:], xnc[dt][:], 1.0,
                 p.pb[0][:, :CHB], ALU.bypass, ALU.subtract)
            p.op("vector", "tensor_mul", tmp[:], tmp[:], p.pb[1][:, :CHB])
            p.op("vector", "tensor_add", x1c[dt][:], x1c[dt][:], tmp[:])
            p.dma("gpsimd", x2T_dr[dt * 128:(dt + 1) * 128, tsl], x1c[dt][:])
    p.release(mark0)



def build_stage_a_program():
    nc = bass.Bass("TRN2", target_bir_lowering=False)
    dr = declare_const_inputs(nc)
    x = nc.dram_tensor("xa", [T, D], F32, kind="ExternalInput")
    ls = nc.dram_tensor("l0_ls", [1, D], F32, kind="ExternalInput")
    lb = nc.dram_tensor("l0_lb", [1, D], F32, kind="ExternalInput")
    wq = nc.dram_tensor("l0_wq", [D, D], F32, kind="ExternalInput")
    wk = nc.dram_tensor("l0_wk", [D, D], F32, kind="ExternalInput")
    wv = nc.dram_tensor("l0_wv", [D, D], F32, kind="ExternalInput")
    x1T = nc.dram_tensor("x1T", [D, T], F32, kind="ExternalOutput")
    scr = nc.dram_tensor("scrA", [514, T], F32, kind="Internal")
    p = Prog(nc)
    cm = CommonSB(p, dr)
    stage_a(p, cm, x, ls, lb, wq, wk, wv, x1T, scr)
    p.barrier()
    p.finalize()
    return nc



def stage_c(p: Prog, cm: CommonSB, x2T_dr, xpw_dr, dtw_dr, dtb_dr, alog_dr,
            dpar_dr, lna_dr, lnb_dr, selb_dr, pf_dr, rb_dr, x3T_dr):
    TC = 512
    NTC = T // TC
    mark0 = p.mark()

    m_right = p.mark("right")
    xpw_st = p.sb([96, D], F32, side="right")
    xpw_b = p.sb([96, D], BF16, side="right")
    dtw_st = p.sb([128, 64], F32, side="right")
    dtw_b = p.sb([128, 64], BF16, side="right")
    alog_st = p.sb([128, 128], F32, side="right")

    xpwT = [p.sb([128, 96], BF16) for _ in range(ND)]
    dtwT = [p.sb([64, 128], BF16) for _ in range(ND)]
    negA = p.sb([128, 128], F32)
    selb = p.sb([128, 16 * 128], BF16)
    pf32 = p.sb([128, 16 * 128], F32)
    rb = p.sb([16, 128], BF16)
    state = p.sb([128, 128], F32)

    p.dma("sync", xpw_st[:], xpw_dr[:])
    p.op("scalar", "activation", out=xpw_b[:], in_=xpw_st[:], func=AF.Copy)
    for din in range(ND):
        bank = din % 2
        p.op("tensor", "transpose", p.pbh[bank][:, 0:96],
             xpw_b[:, din * 128:(din + 1) * 128], cm.identb[0:96, 0:96])
        p.op("vector", "tensor_copy", xpwT[din][:], p.pbh[bank][:, 0:96])
    for dt in range(ND):
        bank = dt % 2
        p.dma("sync", dtw_st[:], dtw_dr[dt * 128:(dt + 1) * 128, :])
        p.op("scalar", "activation", out=dtw_b[:], in_=dtw_st[:], func=AF.Copy)
        p.op("tensor", "transpose", p.pbh[bank][0:64, 0:128], dtw_b[:], cm.identb[:])
        p.op("vector", "tensor_copy", dtwT[dt][:], p.pbh[bank][0:64, 0:128])
    p.dma("sync", alog_st[:], alog_dr.rearrange("(k x) n -> k (x n)", x=8))
    p.op("tensor", "transpose", p.pb[0][:, 0:128], alog_st[:], cm.identf[:])
    p.op("scalar", "activation", out=negA[:], in_=p.pb[0][:, 0:128], func=AF.Exp)
    p.op("vector", "tensor_scalar", negA[:], negA[:], -1.0, None, ALU.mult)
    p.op("vector", "memset", state[:], 0.0)
    for j in range(16):
        p.dma("sync", selb[:, j * 128:(j + 1) * 128], selb_dr[j * 128:(j + 1) * 128, :])
        p.dma("sync", pf32[:, j * 128:(j + 1) * 128], pf_dr[j * 128:(j + 1) * 128, :])
    p.dma("sync", rb[:], rb_dr[:])
    dcol = col_layout(p, cm, dpar_dr, ND)
    dtbcol = col_layout(p, cm, dtb_dr, ND)
    acol = col_layout(p, cm, lna_dr, ND)
    bcol = col_layout(p, cm, lnb_dr, ND)
    p.release(m_right, "right")

    x2c = [p.sb([128, TC], F32) for _ in range(ND)]
    xln = [p.sb([128, TC], F32) for _ in range(ND)]
    xln_b = [p.sb([128, TC], BF16) for _ in range(ND)]
    delta = [p.sb([128, TC], F32) for _ in range(ND)]
    dhi = [p.sb([128, TC], BF16) for _ in range(ND)]
    dlo = [p.sb([128, TC], BF16) for _ in range(ND)]
    uhi = [p.sb([128, TC], BF16) for _ in range(ND)]
    ulo = [p.sb([128, TC], BF16) for _ in range(ND)]
    u_f = p.sb([128, TC], F32)
    drhi = p.sb([64, TC], BF16)
    drlo = p.sb([64, TC], BF16)
    bm_f = p.sb([16, TC], F32)
    bmhi = p.sb([16, TC], BF16)
    bmlo = p.sb([16, TC], BF16)
    cmhi = p.sb([16, TC], BF16)
    cmlo = p.sb([16, TC], BF16)
    bmrep = p.sb([128, TC], F32)
    cmrep = p.sb([128, TC], F32)
    sp1 = [p.sb([128, TC], F32) for _ in range(2)]
    dA = [p.sb([128, TC], F32) for _ in range(3)]
    u_sb = [p.sb([128, TC], F32) for _ in range(3)]
    bx = [p.sb([128, TC], F32) for _ in range(3)]
    hh = [p.sb([128, TC], F32) for _ in range(3)]
    hc = [p.sb([128, TC], F32) for _ in range(3)]
    y_sb = [p.sb([128, TC], F32) for _ in range(ND)]
    sso = [p.sb([128, TC], F32) for _ in range(ND)]
    tmp = p.sb([128, TC], F32)
    ln = FeatLN(p, cm, TC)

    for cc in range(NTC):
        tsl = slice(cc * TC, (cc + 1) * TC)
        for j in range(ND):
            p.dma("sync", x2c[j][:], x2T_dr[j * 128:(j + 1) * 128, tsl])
        ln.stats(x2c, 0, 1)
        ln.rstd_plain(2)
        ln.bcast(0, 1)
        ln.apply_plain(x2c, [t[:] for t in xln], 0, 1, tmp)
        for j in range(ND):
            p.op("scalar", "activation", out=xln_b[j][:], in_=xln[j][:], func=AF.Copy)
        for din in range(ND):
            st = (din == 0)
            sp = (din == ND - 1)
            p.op("tensor", "matmul", p.pb[0][0:64, :], xpwT[din][:, 0:64], xln_b[din][:],
                 start=st, stop=sp)
            p.op("tensor", "matmul", p.pb[1][0:16, :], xpwT[din][:, 64:80], xln_b[din][:],
                 start=st, stop=sp, skip_group_check=True)
            p.op("tensor", "matmul", p.pb[2][0:16, :], xpwT[din][:, 80:96], xln_b[din][:],
                 start=st, stop=sp, skip_group_check=True)
        p.op("scalar", "activation", out=drhi[:], in_=p.pb[0][0:64, :], func=AF.Copy)
        p.op("vector", "tensor_sub", drlo[:], p.pb[0][0:64, :], drhi[:])
        p.op("scalar", "activation", out=bmhi[:], in_=p.pb[1][0:16, :], func=AF.Copy)
        p.op("vector", "tensor_sub", bmlo[:], p.pb[1][0:16, :], bmhi[:])
        p.op("scalar", "activation", out=cmhi[:], in_=p.pb[2][0:16, :], func=AF.Copy)
        p.op("vector", "tensor_sub", cmlo[:], p.pb[2][0:16, :], cmhi[:])
        p.op("tensor", "matmul", p.pb[5][:, :], rb[:], bmhi[:], start=True, stop=False)
        p.op("tensor", "matmul", p.pb[5][:, :], rb[:], bmlo[:], start=False, stop=True)
        p.op("vector", "tensor_copy", bmrep[:], p.pb[5][:, :])
        p.op("tensor", "matmul", p.pb[5][:, :], rb[:], cmhi[:], start=True, stop=False)
        p.op("tensor", "matmul", p.pb[5][:, :], rb[:], cmlo[:], start=False, stop=True)
        p.op("vector", "tensor_copy", cmrep[:], p.pb[5][:, :])
        for dt in range(ND):
            bank = 3 + dt % 2
            p.op("tensor", "matmul", p.pb[bank][:, :], dtwT[dt][:], drhi[:],
                 start=True, stop=False)
            p.op("tensor", "matmul", p.pb[bank][:, :], dtwT[dt][:], drlo[:],
                 start=False, stop=True)
            s1 = sp1[dt % 2]
            p.op("scalar", "activation", out=s1[:], in_=p.pb[bank][:, :], func=AF.Abs,
                 bias=dtbcol[:, dt:dt + 1])
            p.op("scalar", "activation", out=s1[:], in_=s1[:], func=AF.Exp, scale=cm.c(8))
            p.op("scalar", "activation", out=s1[:], in_=s1[:], func=AF.Ln, bias=cm.c(5))
            p.op("scalar", "activation", out=delta[dt][:], in_=p.pb[bank][:, :], func=AF.Relu,
                 bias=dtbcol[:, dt:dt + 1])
            p.op("vector", "tensor_add", delta[dt][:], delta[dt][:], s1[:])
            p.op("scalar", "activation", out=dhi[dt][:], in_=delta[dt][:], func=AF.Copy)
            p.op("vector", "tensor_sub", dlo[dt][:], delta[dt][:], dhi[dt][:])
            p.op("vector", "tensor_mul", u_f[:], delta[dt][:], xln[dt][:])
            p.op("scalar", "activation", out=uhi[dt][:], in_=u_f[:], func=AF.Copy)
            p.op("vector", "tensor_sub", ulo[dt][:], u_f[:], uhi[dt][:])
        for k in range(128):
            dt, jj = k // 16, k % 16
            bdr = k % 2
            bur = 2 + k % 2
            by = 6 + dt % 2
            sel = selb[:, jj * 128:(jj + 1) * 128]
            p.op("tensor", "matmul", p.pb[bdr][:, :], sel, dhi[dt][:], start=True, stop=False)
            p.op("tensor", "matmul", p.pb[bdr][:, :], sel, dlo[dt][:], start=False, stop=True)
            p.op("scalar", "activation", out=dA[k % 3][:], in_=p.pb[bdr][:, :], func=AF.Exp,
                 scale=negA[:, k:k + 1])
            p.op("tensor", "matmul", p.pb[bur][:, :], sel, uhi[dt][:], start=True, stop=False)
            p.op("tensor", "matmul", p.pb[bur][:, :], sel, ulo[dt][:], start=False, stop=True)
            p.op("scalar", "activation", out=u_sb[k % 3][:], in_=p.pb[bur][:, :], func=AF.Copy)
            p.op("gpsimd", "tensor_mul", bx[k % 3][:], u_sb[k % 3][:], bmrep[:])
            p.op("vector", "tensor_tensor_scan", hh[k % 3][:], dA[k % 3][:], bx[k % 3][:],
                 state[:, k:k + 1], ALU.mult, ALU.add)
            p.op("vector", "tensor_copy", state[:, k:k + 1], hh[k % 3][:, TC - 1:TC])
            p.op("vector", "tensor_mul", hc[k % 3][:], hh[k % 3][:], cmrep[:])
            p.op("tensor", "matmul", p.pb[by][:, :], pf32[:, jj * 128:(jj + 1) * 128],
                 hc[k % 3][:], start=(jj == 0), stop=(jj == 15))
            if jj == 15:
                p.op("vector", "tensor_scalar", tmp[:], xln[dt][:], dcol[:, dt:dt + 1],
                     None, ALU.mult)
                p.op("vector", "tensor_add", y_sb[dt][:], p.pb[by][:, :], tmp[:])
        ln.stats(y_sb, 0, 1)
        p.op("vector", "tensor_scalar", ln.rowQ[:], ln.rowQ[:],
             1024.0 / 1023.0, None, ALU.mult)
        p.op("scalar", "activation", out=ln.rowQ[:], in_=ln.rowQ[:], func=AF.Ln)
        p.op("scalar", "activation", out=ln.rowQ[:], in_=ln.rowQ[:], func=AF.Exp,
             scale=cm.c1(9))
        p.op("vector", "tensor_scalar", ln.rowQ[:], ln.rowQ[:], 1e-12, None, ALU.add)
        p.op("scalar", "activation", out=ln.rowQ[:], in_=ln.rowQ[:], func=AF.Ln)
        p.op("scalar", "activation", out=ln.rowQ[:], in_=ln.rowQ[:], func=AF.Exp,
             scale=cm.c1(8))
        ln.bcast(0, 1)
        for dt in range(ND):
            p.op("vector", "scalar_tensor_tensor", tmp[:], y_sb[dt][:], 1.0,
                 p.pb[0][:, :], ALU.bypass, ALU.subtract)
            p.op("vector", "tensor_mul", tmp[:], tmp[:], p.pb[1][:, :])
            p.op("vector", "tensor_scalar", sso[dt][:], tmp[:], acol[:, dt:dt + 1],
                 bcol[:, dt:dt + 1], ALU.mult, ALU.add)
        ln.stats(sso, 2, 3)
        ln.rstd_plain(2)
        ln.bcast(2, 3)
        for dt in range(ND):
            p.op("vector", "scalar_tensor_tensor", tmp[:], sso[dt][:], 1.0,
                 p.pb[2][:, :], ALU.bypass, ALU.subtract)
            p.op("vector", "tensor_mul", tmp[:], tmp[:], p.pb[3][:, :])
            p.op("vector", "tensor_add", x2c[dt][:], x2c[dt][:], tmp[:])
            p.dma("gpsimd", x3T_dr[dt * 128:(dt + 1) * 128, tsl], x2c[dt][:])
    p.release(mark0)


def host_ssm_consts():
    sel = np.zeros((16, 128, 128), np.float32)
    for j in range(16):
        for pp in range(128):
            sel[j, 8 * j + pp // 16, pp] = 1.0
    pf = np.transpose(sel, (0, 2, 1)).reshape(2048, 128).astype(np.float32)
    selb = sel.reshape(2048, 128).astype(BF)
    rbm = np.zeros((16, 128), np.float32)
    for pp in range(128):
        rbm[pp % 16, pp] = 1.0
    return {"selb": selb, "pf": pf, "rb": rbm.astype(BF)}



W1F = 5460
F2 = 2730
NF2 = 22
CHA = 512
CHF = 256


def f2w(i):
    return min(128, F2 - 128 * i)


def hb2ftb(cm, p):
    return p.hb2ftb_t[:]


def col_layout2(p, cm, vec_ap, n, name=None):
    rows = p.sb([1, 128 * n], F32, side="right")
    cols = p.sb([128, n], F32, name=name)
    p.dma("sync", rows[:], vec_ap)
    for j in range(n):
        bank = j % 2
        p.op("tensor", "transpose", p.pb[bank][:, 0:1], rows[0:1, j * 128:(j + 1) * 128],
             cm.identf[0:1, 0:1])
        p.op("vector", "tensor_copy", cols[:, j:j + 1], p.pb[bank][:, 0:1])
    return cols


def stage_d(p: Prog, cm: CommonSB, dd, x3T_dr, y_dr, ping_dr, pong_dr, nlayers=6):
    mark0 = p.mark()
    masks = p.sb([128, 4 * 256], BF16)
    hb2b = p.sb([128, 2], BF16)
    hb2f = p.sb([128, 2], F32)
    hb2ft = p.sb([2, 128], F32)
    hb2ftb_t = p.sb([2, 128], BF16)
    for i in range(4):
        p.dma("sync", masks[:, i * 256:(i + 1) * 256], dd["masks"][i * 128:(i + 1) * 128, :])
    p.dma("sync", hb2b[:], dd["hb2b"][:])
    p.dma("sync", hb2f[:], dd["hb2f"][:])
    p.dma("sync", hb2ft[:], dd["hb2ft"][:])
    p.dma("sync", hb2ftb_t[:], dd["hb2ftb"][:])
    p.hb2ftb_t = hb2ftb_t
    bones64 = p.sb([128, 128], BF16)
    p.dma("sync", bones64[:], dd["bones64"][:])
    p.bones64 = bones64

    m_pre = p.mark()
    xc = [p.sb([128, CHA], F32) for _ in range(ND)]
    tmp0 = p.sb([128, CHA], F32)
    lnp = FeatLN(p, cm, CHA)
    for cc in range(NCH):
        tsl = slice(cc * CHA, (cc + 1) * CHA)
        for j in range(ND):
            p.dma("sync", xc[j][:], x3T_dr[j * 128:(j + 1) * 128, tsl])
        lnp.stats(xc, 0, 1)
        lnp.rstd_plain(2)
        lnp.bcast(0, 1)
        lnp.apply_plain(xc, [t[:] for t in xc], 0, 1, tmp0)
        for j in range(ND):
            p.dma("gpsimd", ping_dr[j * 128:(j + 1) * 128, tsl], xc[j][:])
    p.release(m_pre)

    src_dr, dst_dr = ping_dr, pong_dr
    for ll in range(nlayers):
        attn_sublayer(p, cm, dd, ll, src_dr, dst_dr, masks, hb2f, hb2ft)
        src_dr, dst_dr = dst_dr, src_dr
        ffn_sublayer(p, cm, dd, ll, src_dr, dst_dr)
        src_dr, dst_dr = dst_dr, src_dr

    m_post = p.mark()
    xc = [p.sb([128, CHA], F32) for _ in range(ND)]
    sc = [p.sb([128, CHA], F32) for _ in range(ND)]
    otile = [p.sb([128, D], F32) for _ in range(2)]
    tmp0 = p.sb([128, CHA], F32)
    lnp = FeatLN(p, cm, CHA)
    for cc in range(NCH):
        tsl = slice(cc * CHA, (cc + 1) * CHA)
        for j in range(ND):
            p.dma("sync", xc[j][:], x3T_dr[j * 128:(j + 1) * 128, tsl])
            p.dma("sync", sc[j][:], src_dr[j * 128:(j + 1) * 128, tsl])
        lnp.stats(sc, 0, 1)
        lnp.rstd_plain(2)
        lnp.bcast(0, 1)
        lnp.apply_plain(sc, [t[:] for t in sc], 0, 1, tmp0, add_to=xc)
        for q4 in range(4):
            ot = otile[q4 % 2]
            for j in range(ND):
                bank = 2 + (q4 * ND + j) % 2
                p.op("tensor", "transpose", p.pb[bank][:, 0:128],
                     sc[j][:, q4 * 128:(q4 + 1) * 128], cm.identf[:])
                p.op("vector", "tensor_copy", ot[:, j * 128:(j + 1) * 128],
                     p.pb[bank][:, 0:128])
            p.dma("gpsimd", y_dr[cc * CHA + q4 * 128: cc * CHA + (q4 + 1) * 128, :], ot[:])
    p.release(m_post)
    p.release(mark0)


def attn_sublayer(p, cm, dd, ll, src_dr, dst_dr, masks, hb2f, hb2ft):
    CH2 = 256
    NC2 = T // CH2
    m_w = p.mark()
    wst = [p.sb([128, 512], F32) for _ in range(4)]
    wqkvb = [p.sb([128, 3072], BF16) for _ in range(ND)]
    wob = [p.sb([128, D], BF16) for _ in range(ND)]
    it = 0
    for s6 in range(6):
        for din in range(ND):
            p.dma("scalar", wst[it % 4][:], dd["wqkv"][ll, din * 128:(din + 1) * 128,
                                                       s6 * 512:(s6 + 1) * 512])
            p.op("gpsimd", "tensor_copy", wqkvb[din][:, s6 * 512:(s6 + 1) * 512],
                 wst[it % 4][:])
            it += 1
    for s2 in range(2):
        for din in range(ND):
            p.dma("scalar", wst[it % 4][:], dd["wo"][ll, din * 128:(din + 1) * 128,
                                                     s2 * 512:(s2 + 1) * 512])
            p.op("gpsimd", "tensor_copy", wob[din][:, s2 * 512:(s2 + 1) * 512],
                 wst[it % 4][:])
            it += 1
    qsr = p.sb([1, 64], F32)
    ksr = p.sb([1, 64], F32)
    qsr_s = p.sb([1, 64], F32)
    ksr_s = p.sb([1, 64], F32)
    r1 = p.sb([1, 64], F32)
    r2 = p.sb([1, 1], F32)
    qscol = p.sb([128, 1], F32)
    kscol = p.sb([128, 1], F32)
    qscol_s = p.sb([128, 1], F32)
    kscol_s = p.sb([128, 1], F32)
    shcol = p.sb([128, 1], F32)
    p.dma("sync", qsr[:], dd["qs"][ll:ll + 1, :])
    p.dma("sync", ksr[:], dd["ks"][ll:ll + 1, :])
    for sr, dsr in ((qsr, qsr_s), (ksr, ksr_s)):
        p.op("vector", "tensor_copy", dsr[:, 0:32], sr[:, 32:64])
        p.op("vector", "tensor_copy", dsr[:, 32:64], sr[:, 0:32])
    for dst_c, src_r in ((qscol, qsr), (kscol, ksr), (qscol_s, qsr_s), (kscol_s, ksr_s)):
        p.op("tensor", "transpose", p.pb[6][0:64, 0:1], src_r[:], cm.identf[0:1, 0:1])
        p.op("vector", "tensor_copy", dst_c[0:64, 0:1], p.pb[6][0:64, 0:1])
        p.op("vector", "tensor_copy", dst_c[64:128, 0:1], p.pb[6][0:64, 0:1])
    p.op("vector", "tensor_mul", r1[:], qsr[:], ksr[:])
    p.op("vector", "reduce_max", r2[:], r1[:], axis=AX.X, apply_absolute_value=True)
    p.op("vector", "tensor_scalar", r2[:], r2[:], -8.0, 80.0, ALU.mult, ALU.add)
    p.op("vector", "tensor_scalar_min", r2[:], r2[:], 0.0)
    p.op("tensor", "matmul", p.pb[6][:, 0:1], cm.ones_row_f[:], r2[:], start=True, stop=True)
    p.op("vector", "tensor_copy", shcol[:], p.pb[6][:, 0:1])

    xbuf = [[p.sb([128, CH2 + 1], F32) for _ in range(ND)] for _ in range(2)]
    xb_b = [[p.sb([128, CH2 + 1], BF16) for _ in range(ND)] for _ in range(2)]
    rot_f = p.sb([128, 8 * CH2], F32)
    rot_sh = [p.sb([128, CH2], F32) for _ in range(2)]
    rrep_q = p.sb([128, 8 * CH2], F32)
    q_b = [p.sb([128, 8 * CH2], BF16) for _ in range(2)]
    qp = [p.sb([128, CH2], BF16) for _ in range(2)]
    k_b = [p.sb([128, 8 * CH2], BF16) for _ in range(3)]
    att_b = p.sb([128, 8 * CH2], BF16)
    V_sb = [[p.sb([128, 16 * 128], BF16) for _ in range(2)] for _ in range(3)]
    tabs = [p.sb([128, CH2], F32) for _ in range(4)]
    sq_f = p.sb([128, CH2], F32)
    sqh_b = p.sb([128, CH2], BF16)
    sql_b = p.sb([128, CH2], BF16)
    tmpA = p.sb([128, CH2], F32)
    vt_b = p.sb([128, CH2], BF16)
    probs2 = [[p.sb([128, 512], BF16) for _ in range(2)] for _ in range(2)]
    den_r = [p.sb([1, CH2], F32) for _ in range(2)]
    den_h = [p.sb([1, CH2], BF16) for _ in range(2)]
    den_l = [p.sb([1, CH2], BF16) for _ in range(2)]

    def emit_front(cc):
        par2 = cc % 2
        par3 = cc % 3
        t0 = cc * CH2
        for j in range(ND):
            if cc == 0:
                p.op("vector", "memset", xbuf[par2][j][:, 0:1], 0.0)
                p.dma("sync", xbuf[par2][j][:, 1:CH2 + 1], src_dr[j * 128:(j + 1) * 128, 0:CH2])
            else:
                p.dma("sync", xbuf[par2][j][:], src_dr[j * 128:(j + 1) * 128, t0 - 1:t0 + CH2])
            p.op("scalar", "activation", out=xb_b[par2][j][:], in_=xbuf[par2][j][:],
                 func=AF.Copy)

        def rhs(din):
            return (xb_b[par2][din][:, 1:CH2 + 1] if din < 4
                    else xb_b[par2][din][:, 0:CH2])

        for ti, key, col in ((0, "tab_cq", qscol), (1, "tab_sq", qscol_s),
                             (2, "tab_ck", kscol), (3, "tab_sk", kscol_s)):
            p.dma("sync", tabs[ti][:], dd[key][:, t0:t0 + CH2])
            p.op("vector", "tensor_scalar", tabs[ti][:], tabs[ti][:], col[:, 0:1],
                 None, ALU.mult)

        for is_k in (False, True):
            coff = 1024 if is_k else 0
            for dt in range(ND):
                bank = 6 + dt % 2
                csl = slice(dt * CH2, (dt + 1) * CH2)
                for din in range(ND):
                    p.op("tensor", "matmul", p.pb[bank][:, 0:CH2],
                         wqkvb[din][:, coff + dt * 128: coff + (dt + 1) * 128],
                         rhs(din), start=(din == 0), stop=(din == ND - 1))
                p.op("scalar", "activation", out=rot_f[:, csl], in_=p.pb[bank][:, 0:CH2],
                     func=AF.Copy)
                p.op("scalar", "activation", out=sqh_b[:], in_=p.pb[bank][:, 0:CH2],
                     func=AF.Square)
                p.op("scalar", "activation", out=sq_f[:], in_=p.pb[bank][:, 0:CH2],
                     func=AF.Square)
                p.op("vector", "tensor_sub", sql_b[:], sq_f[:], sqh_b[:])
                p.op("tensor", "matmul", p.pb[bank][:, 256:256 + CH2], p.bones64[:], sqh_b[:],
                     start=True, stop=False, skip_group_check=True)
                p.op("tensor", "matmul", p.pb[bank][:, 256:256 + CH2], p.bones64[:], sql_b[:],
                     start=False, stop=True, skip_group_check=True)
                p.op("vector", "tensor_scalar_max", sq_f[:], p.pb[bank][:, 256:256 + CH2],
                     1e-24)
                p.op("scalar", "activation", out=sq_f[:], in_=sq_f[:], func=AF.Ln)
                if is_k:
                    p.op("scalar", "activation", out=rrep_q[:, csl], in_=sq_f[:], func=AF.Exp,
                         scale=cm.c(4), bias=cm.c(14))
                else:
                    p.op("scalar", "activation", out=rrep_q[:, csl], in_=sq_f[:], func=AF.Exp,
                         scale=cm.c(4))
            def shuf(dt):
                csl = slice(dt * CH2, (dt + 1) * CH2)
                for a in range(2):
                    for b_ in range(2):
                        p.dma("scalar",
                              rot_sh[dt % 2][a * 64 + b_ * 32: a * 64 + b_ * 32 + 32, :],
                              rot_f[a * 64 + (1 - b_) * 32: a * 64 + (1 - b_) * 32 + 32, csl])
            shuf(0)
            ct, st_ = (tabs[2], tabs[3]) if is_k else (tabs[0], tabs[1])
            for dt in range(ND):
                if dt + 1 < ND:
                    shuf(dt + 1)
                csl = slice(dt * CH2, (dt + 1) * CH2)
                p.op("vector", "tensor_mul", tmpA[:], rot_f[:, csl], ct[:])
                p.op("vector", "tensor_mul", rot_f[:, csl], rot_sh[dt % 2][:], st_[:])
                p.op("vector", "tensor_add", rot_f[:, csl], rot_f[:, csl], tmpA[:])
                dst_qb = k_b[par3] if is_k else q_b[par2]
                p.op("vector", "tensor_mul", dst_qb[:, csl], rot_f[:, csl],
                     rrep_q[:, csl])
        for dt in range(ND):
            bank = 6 + dt % 2
            for din in range(ND):
                p.op("tensor", "matmul", p.pb[bank][:, 0:CH2],
                     wqkvb[din][:, 2048 + dt * 128: 2048 + (dt + 1) * 128],
                     rhs(din), start=(din == 0), stop=(din == ND - 1))
            p.op("scalar", "activation", out=vt_b[:], in_=p.pb[bank][:, 0:CH2], func=AF.Copy)
            for tb in range(2):
                tbk = 6 + (dt * 2 + tb + 1) % 2
                p.op("tensor", "transpose", p.pbh[tbk][:, 0:128],
                     vt_b[:, tb * 128:(tb + 1) * 128], cm.identb[:])
                vv = V_sb[par3][tb][:].rearrange("p (h c) -> p h c", c=128)
                p.op("vector", "tensor_copy", vv[:, 2 * dt:2 * dt + 2, 0:64],
                     p.pbh[tbk][:, 0:128].rearrange("p (h c) -> p h c", c=64))
        for tb in range(2):
            vv = V_sb[par3][tb][:].rearrange("p (h c) -> p h c", c=128)
            p.op("vector", "memset", vv[:, :, 64:65], 1.0)
            p.op("vector", "memset", vv[:, :, 65:128], 0.0)

    def emit_attn(cc):
        par2 = cc % 2
        par3 = cc % 3
        prev3 = (cc - 1) % 3
        t0 = cc * CH2
        kjs = []
        for pos in range(4):
            gt = cc * 2 - 2 + pos
            if gt < 0:
                continue
            kjs.append((pos, prev3 if pos < 2 else par3, gt % 2))

        def d_scores(h):
            ht, hb = h // 2, 64 * (h % 2)
            qsl = slice(ht * CH2, (ht + 1) * CH2)
            g0 = (h % 2) * 2
            p.op("vector", "tensor_copy", qp[h % 2][hb:hb + 64, :],
                 q_b[par2][hb:hb + 64, qsl])
            for idx, (pos, kpar, tb) in enumerate(kjs):
                bank = g0 + idx // 2
                off = (idx % 2) * 256
                ksl = slice(ht * CH2 + tb * 128, ht * CH2 + tb * 128 + 128)
                p.op("tensor", "matmul", p.pb[bank][:, off:off + 256],
                     k_b[kpar][:, ksl], qp[h % 2][:],
                     start=(idx % 2 == 0), stop=False, skip_group_check=True)
            for pi in range((len(kjs) + 1) // 2):
                bank = g0 + pi
                pos0 = kjs[2 * pi][0]
                p.op("tensor", "matmul", p.pb[bank][:, :],
                     cm.identb[:], masks[:, pos0 * 256: pos0 * 256 + 512],
                     start=False, stop=True, skip_group_check=True)
                p.op("scalar", "activation", out=probs2[h % 2][pi][:],
                     in_=p.pb[bank][:, :], func=AF.Exp, bias=shcol[:, 0:1])

        def d_pv_tail(h):
            ht, hb = h // 2, 64 * (h % 2)
            pvbank = 4 + h % 2
            for idx, (pos, kpar, tb) in enumerate(kjs):
                off = (idx % 2) * 256
                vv = V_sb[kpar][tb][:].rearrange("p (hx c) -> p hx c", c=128)[:, h, :]
                p.op("tensor", "matmul", p.pb[pvbank][:, 0:CH2], vv,
                     probs2[h % 2][idx // 2][:, off:off + 256],
                     start=(idx == 0), stop=(idx == len(kjs) - 1),
                     skip_group_check=True)
            p.op("scalar", "activation", out=den_r[h % 2][:],
                 in_=p.pb[pvbank][64:65, 0:CH2], func=AF.Ln)
            p.op("scalar", "activation", out=den_r[h % 2][:], in_=den_r[h % 2][:],
                 func=AF.Exp, scale=cm.c1(8))
            asl = slice(ht * CH2, (ht + 1) * CH2)
            p.op("vector", "tensor_copy", att_b[hb:hb + 64, asl],
                 p.pb[pvbank][0:64, 0:CH2])
            p.op("scalar", "activation", out=den_h[h % 2][:], in_=den_r[h % 2][:],
                 func=AF.Copy)
            p.op("vector", "tensor_sub", den_l[h % 2][:], den_r[h % 2][:], den_h[h % 2][:])
            if h % 2 == 1:
                dvb = (h % 2) * 2
                p.op("tensor", "matmul", p.pb[dvb][0:64, 0:CH2],
                     cm.ones_row_b[:, 0:64], den_h[0][:], start=True, stop=False,
                     skip_group_check=True)
                p.op("tensor", "matmul", p.pb[dvb][0:64, 0:CH2],
                     cm.ones_row_b[:, 0:64], den_l[0][:], start=False, stop=True,
                     skip_group_check=True)
                p.op("tensor", "matmul", p.pb[dvb][64:128, 0:CH2],
                     cm.ones_row_b[:, 0:64], den_h[1][:], start=True, stop=False,
                     skip_group_check=True)
                p.op("tensor", "matmul", p.pb[dvb][64:128, 0:CH2],
                     cm.ones_row_b[:, 0:64], den_l[1][:], start=False, stop=True,
                     skip_group_check=True)
                p.op("vector", "tensor_mul", att_b[:, asl], att_b[:, asl],
                     p.pb[dvb][:, 0:CH2])

        d_scores(0)
        for h in range(1, 16):
            d_scores(h)
            d_pv_tail(h - 1)
        d_pv_tail(15)
        for dout in range(ND):
            bank = 6 + dout % 2
            for din in range(ND):
                p.op("tensor", "matmul", p.pb[bank][:, 0:CH2],
                     wob[din][:, dout * 128:(dout + 1) * 128],
                     att_b[:, din * CH2:(din + 1) * CH2],
                     start=(din == 0), stop=(din == ND - 1))
            p.op("vector", "tensor_add", xbuf[par2][dout][:, 1:CH2 + 1],
                 xbuf[par2][dout][:, 1:CH2 + 1], p.pb[bank][:, 0:CH2])
            p.dma("gpsimd", dst_dr[dout * 128:(dout + 1) * 128, t0:t0 + CH2],
                  xbuf[par2][dout][:, 1:CH2 + 1])

    p.op("vector", "memset", qp[0][64:128, :], 0.0)
    p.op("vector", "memset", qp[1][0:64, :], 0.0)
    emit_front(0)
    for cc in range(1, NC2):
        emit_front(cc)
        emit_attn(cc - 1)
    emit_attn(NC2 - 1)
    p.release(m_w)


def ffn_sublayer(p, cm, dd, ll, src_dr, dst_dr):
    m_w = p.mark()
    wst = [p.sb([128, 512], F32) for _ in range(4)]
    w1b = [p.sb([128, W1F], BF16) for _ in range(ND)]
    w2b = [p.sb([128, D], BF16) for _ in range(NF2)]
    it = 0
    for s in range(11):
        wdt = min(512, W1F - 512 * s)
        for din in range(ND):
            p.dma("scalar", wst[it % 4][:, 0:wdt],
                  dd["w1"][ll, din * 128:(din + 1) * 128, 512 * s:512 * s + wdt])
            p.op("gpsimd", "tensor_copy", w1b[din][:, 512 * s:512 * s + wdt],
                 wst[it % 4][:, 0:wdt])
            it += 1
    for i in range(NF2):
        w = f2w(i)
        for s2 in range(2):
            p.dma("scalar", wst[it % 4][0:w, :], dd["w2"][ll, 128 * i:128 * i + w,
                                                          s2 * 512:(s2 + 1) * 512])
            p.op("gpsimd", "tensor_copy", w2b[i][0:w, s2 * 512:(s2 + 1) * 512],
                 wst[it % 4][0:w, :])
            it += 1
    m_right = p.mark("right")
    b1row = p.sb([1, W1F], F32, side="right")
    b1ac = p.sb([128, NF2], F32)
    b1gc = p.sb([128, NF2], F32)
    p.dma("sync", b1row[:], dd["b1"][ll:ll + 1, :])
    for i in range(NF2):
        w = f2w(i)
        p.op("tensor", "transpose", p.pb[6][0:w, 0:1], b1row[0:1, 128 * i:128 * i + w],
             cm.identf[0:1, 0:1])
        p.op("vector", "tensor_copy", b1ac[0:w, i:i + 1], p.pb[6][0:w, 0:1])
        p.op("tensor", "transpose", p.pb[7][0:w, 0:1],
             b1row[0:1, F2 + 128 * i: F2 + 128 * i + w], cm.identf[0:1, 0:1])
        p.op("vector", "tensor_copy", b1gc[0:w, i:i + 1], p.pb[7][0:w, 0:1])
    gcol = col_layout2(p, cm, dd["lng"][ll:ll + 1, :], ND)
    bcol = col_layout2(p, cm, dd["lnb"][ll:ll + 1, :], ND)
    b2col = col_layout2(p, cm, dd["b2"][ll:ll + 1, :], ND)
    p.release(m_right, "right")

    xbuf = [[p.sb([128, CHF + 1], F32) for _ in range(ND)] for _ in range(2)]
    xn_b = [[p.sb([128, CHF], BF16) for _ in range(ND)] for _ in range(2)]
    y_b = [p.sb([128, CHF], BF16) for _ in range(2)]
    g2_b = [p.sb([128, CHF], BF16) for _ in range(2)]
    tmp = [p.sb([128, CHF], F32) for _ in range(2)]
    ln = [FeatLN(p, cm, CHF) for _ in range(2)]
    NCF = T // CHF

    def emit_front(cc):
        par = cc % 2
        t0 = cc * CHF
        for j in range(ND):
            if cc == 0:
                p.op("vector", "memset", xbuf[par][j][:, 0:1], 0.0)
                p.dma("sync", xbuf[par][j][:, 1:CHF + 1], src_dr[j * 128:(j + 1) * 128, 0:CHF])
            else:
                p.dma("sync", xbuf[par][j][:], src_dr[j * 128:(j + 1) * 128, t0 - 1:t0 + CHF])

        def sview(din):
            return (xbuf[par][din][:, 1:CHF + 1] if din < 4
                    else xbuf[par][din][:, 0:CHF])

        ln[par].stats([sview(j) for j in range(ND)], 6, 7)
        ln[par].rstd_plain(2)
        ln[par].bcast(6, 7)
        for dt in range(ND):
            p.op("vector", "scalar_tensor_tensor", tmp[par][:], sview(dt), 1.0,
                 p.pb[6][:, 0:CHF], ALU.bypass, ALU.subtract)
            p.op("vector", "tensor_mul", tmp[par][:], tmp[par][:], p.pb[7][:, 0:CHF])
            p.op("vector", "tensor_scalar", xn_b[par][dt][:], tmp[par][:],
                 gcol[:, dt:dt + 1], bcol[:, dt:dt + 1], ALU.mult, ALU.add)

    def emit_body(cc):
        par = cc % 2
        t0 = cc * CHF

        def d_w1(i):
            w = f2w(i)
            for din in range(ND):
                p.op("tensor", "matmul", p.pb[4][0:w, 0:CHF],
                     w1b[din][:, 128 * i:128 * i + w], xn_b[par][din][:],
                     start=(din == 0), stop=(din == ND - 1), skip_group_check=True)
                p.op("tensor", "matmul", p.pb[5][0:w, 0:CHF],
                     w1b[din][:, F2 + 128 * i: F2 + 128 * i + w], xn_b[par][din][:],
                     start=(din == 0), stop=(din == ND - 1), skip_group_check=True)
            p.op("scalar", "activation", out=g2_b[i % 2][0:w, :], in_=p.pb[5][0:w, 0:CHF],
                 func=AF.Gelu, bias=b1gc[0:w, i:i + 1])
            p.op("vector", "scalar_tensor_tensor", y_b[i % 2][0:w, :],
                 p.pb[4][0:w, 0:CHF], b1ac[0:w, i:i + 1], g2_b[i % 2][0:w, :],
                 ALU.add, ALU.mult)

        def d_w2(i):
            w = f2w(i)
            for dout in range(ND):
                bz = dout // 2
                zsl = slice((dout % 2) * CHF, (dout % 2) * CHF + CHF)
                p.op("tensor", "matmul", p.pb[bz][:, zsl],
                     w2b[i][0:w, dout * 128:(dout + 1) * 128], y_b[i % 2][0:w, :],
                     start=(i == 0 and dout % 2 == 0), stop=(i == NF2 - 1),
                     skip_group_check=True)
        d_w1(0)
        for i in range(1, NF2):
            d_w1(i)
            d_w2(i - 1)
        d_w2(NF2 - 1)
        for dout in range(ND):
            bz = dout // 2
            zsl = slice((dout % 2) * CHF, (dout % 2) * CHF + CHF)
            p.op("vector", "scalar_tensor_tensor", tmp[par][:], p.pb[bz][:, zsl],
                 b2col[:, dout:dout + 1], xbuf[par][dout][:, 1:CHF + 1], ALU.add, ALU.add)
            p.op("vector", "tensor_copy", xbuf[par][dout][:, 1:CHF + 1], tmp[par][:])
            p.dma("gpsimd", dst_dr[dout * 128:(dout + 1) * 128, t0:t0 + CHF],
                  tmp[par][:])

    emit_front(0)
    for cc in range(1, NCF):
        emit_front(cc)
        emit_body(cc - 1)
    emit_body(NCF - 1)
    p.release(m_w)


def host_l1f_consts():
    t = np.arange(T, dtype=np.float64)
    half = np.arange(0, 64, 2, dtype=np.float64)
    inv_freq = 1.0 / 10000.0 ** (half / 64.0)
    freqs = t[:, None] * inv_freq[None, :]
    base = (half + 0.4 * 64.0) / (1.4 * 64.0)
    power = (t - T // 2) / (256 // 2)
    scale = base[None, :] ** power[:, None]
    cos = np.cos(freqs); sin = np.sin(freqs)
    idx = np.arange(64) % 32
    sign = np.where(np.arange(64) < 32, -1.0, 1.0)
    cq = (cos * scale)[:, idx]
    sq = (sin * scale)[:, idx] * sign[None, :]
    ck = (cos / scale)[:, idx]
    sk = (sin / scale)[:, idx] * sign[None, :]
    pidx = np.arange(128) % 64
    out = {}
    for nm, tab in (("tab_cq", cq), ("tab_sq", sq), ("tab_ck", ck), ("tab_sk", sk)):
        out[nm] = np.ascontiguousarray(tab[:, pidx].T.astype(np.float32))
    i = np.arange(256)[None, :]
    pp = np.arange(128)[:, None]
    allow = [i <= pp, i <= pp + 128, i >= pp, i >= pp + 128]
    msk = np.concatenate([np.where(a, 0.0, -1e7) for a in allow], axis=0)
    out["masks"] = msk.astype(BF)
    hb2 = np.zeros((128, 2), np.float32)
    hb2[0:64, 0] = 1.0
    hb2[64:128, 1] = 1.0
    out["hb2b"] = hb2.astype(BF)
    out["hb2f"] = hb2
    out["hb2ft"] = np.ascontiguousarray(hb2.T)
    out["hb2ftb"] = np.ascontiguousarray(hb2.T).astype(BF)
    bo = np.zeros((128, 128), np.float32)
    bo[0:64, 0:64] = 1.0
    bo[64:128, 64:128] = 1.0
    out["bones64"] = bo.astype(BF)
    return out


def declare_l1f_inputs(nc):
    dd = {}
    dd["wqkv"] = nc.dram_tensor("l1f_wqkv", [6, D, 3072], F32, kind="ExternalInput")
    dd["qs"] = nc.dram_tensor("l1f_qs", [6, 64], F32, kind="ExternalInput")
    dd["ks"] = nc.dram_tensor("l1f_ks", [6, 64], F32, kind="ExternalInput")
    dd["wo"] = nc.dram_tensor("l1f_wo", [6, D, D], F32, kind="ExternalInput")
    dd["lng"] = nc.dram_tensor("l1f_lng", [6, D], F32, kind="ExternalInput")
    dd["lnb"] = nc.dram_tensor("l1f_lnb", [6, D], F32, kind="ExternalInput")
    dd["w1"] = nc.dram_tensor("l1f_w1", [6, D, W1F], F32, kind="ExternalInput")
    dd["b1"] = nc.dram_tensor("l1f_b1", [6, W1F], F32, kind="ExternalInput")
    dd["w2"] = nc.dram_tensor("l1f_w2", [6, F2, D], F32, kind="ExternalInput")
    dd["b2"] = nc.dram_tensor("l1f_b2", [6, D], F32, kind="ExternalInput")
    for nm, shp, dt in (("tab_cq", [128, T], F32), ("tab_sq", [128, T], F32),
                        ("tab_ck", [128, T], F32), ("tab_sk", [128, T], F32),
                        ("masks", [512, 256], BF16), ("hb2b", [128, 2], BF16),
                        ("hb2f", [128, 2], F32), ("hb2ft", [2, 128], F32),
                        ("hb2ftb", [2, 128], BF16), ("bones64", [128, 128], BF16)):
        dd[nm] = nc.dram_tensor(nm, shp, dt, kind="ExternalInput")
    return dd


def build_stage_d_program(nlayers=6):
    nc = bass.Bass("TRN2", target_bir_lowering=False)
    dr = declare_const_inputs(nc)
    dd = declare_l1f_inputs(nc)
    x3T = nc.dram_tensor("x3T", [D, T], F32, kind="ExternalInput")
    y = nc.dram_tensor("y", [T, D], F32, kind="ExternalOutput")
    ping = nc.dram_tensor("pingD", [D, T], F32, kind="Internal")
    pong = nc.dram_tensor("pongD", [D, T], F32, kind="Internal")
    p = Prog(nc)
    cm = CommonSB(p, dr)
    stage_d(p, cm, dd, x3T, y, ping, pong, nlayers=nlayers)
    p.barrier()
    p.finalize()
    return nc


def build_stage_c_program():
    nc = bass.Bass("TRN2", target_bir_lowering=False)
    dr = declare_const_inputs(nc)
    x2T = nc.dram_tensor("x2T", [D, T], F32, kind="ExternalInput")
    xpw = nc.dram_tensor("l1a_xpw", [96, D], F32, kind="ExternalInput")
    dtw = nc.dram_tensor("l1a_dtw", [D, 64], F32, kind="ExternalInput")
    dtb = nc.dram_tensor("l1a_dtb", [D], F32, kind="ExternalInput")
    alog = nc.dram_tensor("l1a_alog", [D, 16], F32, kind="ExternalInput")
    dpar = nc.dram_tensor("l1a_dpar", [D], F32, kind="ExternalInput")
    lna = nc.dram_tensor("l1a_lna", [D], F32, kind="ExternalInput")
    lnb = nc.dram_tensor("l1a_lnb", [D], F32, kind="ExternalInput")
    selb = nc.dram_tensor("selb", [2048, 128], BF16, kind="ExternalInput")
    pf = nc.dram_tensor("pf", [2048, 128], F32, kind="ExternalInput")
    rbt = nc.dram_tensor("rb", [16, 128], BF16, kind="ExternalInput")
    x3T = nc.dram_tensor("x3T", [D, T], F32, kind="ExternalOutput")
    p = Prog(nc)
    cm = CommonSB(p, dr)
    stage_c(p, cm, x2T, xpw, dtw, dtb, alog, dpar, lna, lnb, selb, pf, rbt, x3T)
    p.barrier()
    p.finalize()
    return nc


def build_stage_b_program():
    nc = bass.Bass("TRN2", target_bir_lowering=False)
    dr = declare_const_inputs(nc)
    x1T = nc.dram_tensor("x1T", [D, T], F32, kind="ExternalInput")
    gam = nc.dram_tensor("l0f_g", [D], F32, kind="ExternalInput")
    bet = nc.dram_tensor("l0f_b", [D], F32, kind="ExternalInput")
    q1 = nc.dram_tensor("l0f_q1", [D, 4096], I32, kind="ExternalInput")
    s1 = nc.dram_tensor("l0f_s1", [8, 4096], F32, kind="ExternalInput")
    b1 = nc.dram_tensor("l0f_b1", [1, 4096], F32, kind="ExternalInput")
    q2 = nc.dram_tensor("l0f_q2", [4096, D], I32, kind="ExternalInput")
    s2 = nc.dram_tensor("l0f_s2", [32, D], F32, kind="ExternalInput")
    b2 = nc.dram_tensor("l0f_b2", [D], F32, kind="ExternalInput")
    x2T = nc.dram_tensor("x2T", [D, T], F32, kind="ExternalOutput")
    p = Prog(nc)
    cm = CommonSB(p, dr)
    stage_b(p, cm, x1T, gam, bet, q1, s1, b1, q2, s2, b2, x2T)
    p.barrier()
    p.finalize()
    return nc


def build_full_program():
    nc = bass.Bass("TRN2", target_bir_lowering=False)
    dr = declare_const_inputs(nc)
    x = nc.dram_tensor("xa", [T, D], F32, kind="ExternalInput")
    ls = nc.dram_tensor("l0_ls", [1, D], F32, kind="ExternalInput")
    lb = nc.dram_tensor("l0_lb", [1, D], F32, kind="ExternalInput")
    wq = nc.dram_tensor("l0_wq", [D, D], F32, kind="ExternalInput")
    wk = nc.dram_tensor("l0_wk", [D, D], F32, kind="ExternalInput")
    wv = nc.dram_tensor("l0_wv", [D, D], F32, kind="ExternalInput")
    gam = nc.dram_tensor("l0f_g", [D], F32, kind="ExternalInput")
    bet = nc.dram_tensor("l0f_b", [D], F32, kind="ExternalInput")
    q1 = nc.dram_tensor("l0f_q1", [D, 4096], I32, kind="ExternalInput")
    s1 = nc.dram_tensor("l0f_s1", [8, 4096], F32, kind="ExternalInput")
    b1 = nc.dram_tensor("l0f_b1", [1, 4096], F32, kind="ExternalInput")
    q2 = nc.dram_tensor("l0f_q2", [4096, D], I32, kind="ExternalInput")
    s2 = nc.dram_tensor("l0f_s2", [32, D], F32, kind="ExternalInput")
    b2 = nc.dram_tensor("l0f_b2", [D], F32, kind="ExternalInput")
    xpw = nc.dram_tensor("l1a_xpw", [96, D], F32, kind="ExternalInput")
    dtw = nc.dram_tensor("l1a_dtw", [D, 64], F32, kind="ExternalInput")
    dtb = nc.dram_tensor("l1a_dtb", [D], F32, kind="ExternalInput")
    alog = nc.dram_tensor("l1a_alog", [D, 16], F32, kind="ExternalInput")
    dpar = nc.dram_tensor("l1a_dpar", [D], F32, kind="ExternalInput")
    lna = nc.dram_tensor("l1a_lna", [D], F32, kind="ExternalInput")
    lnb = nc.dram_tensor("l1a_lnb", [D], F32, kind="ExternalInput")
    selb = nc.dram_tensor("selb", [2048, 128], BF16, kind="ExternalInput")
    pf = nc.dram_tensor("pf", [2048, 128], F32, kind="ExternalInput")
    rbt = nc.dram_tensor("rb", [16, 128], BF16, kind="ExternalInput")
    dd = declare_l1f_inputs(nc)
    y = nc.dram_tensor("y", [T, D], F32, kind="ExternalOutput")
    x1T = nc.dram_tensor("x1T", [D, T], F32, kind="Internal")
    x2T = nc.dram_tensor("x2T", [D, T], F32, kind="Internal")
    x3T = nc.dram_tensor("x3T", [D, T], F32, kind="Internal")
    scrA = nc.dram_tensor("scrA", [514, T], F32, kind="Internal")
    ping = nc.dram_tensor("pingD", [D, T], F32, kind="Internal")
    pong = nc.dram_tensor("pongD", [D, T], F32, kind="Internal")

    p = Prog(nc)
    cm = CommonSB(p, dr)
    stage_a(p, cm, x, ls, lb, wq, wk, wv, x1T, scrA)
    stage_b(p, cm, x1T, gam, bet, q1, s1, b1, q2, s2, b2, x2T)
    stage_c(p, cm, x2T, xpw, dtw, dtb, alog, dpar, lna, lnb, selb, pf, rbt, x3T)
    stage_d(p, cm, dd, x3T, y, ping, pong, nlayers=6)
    p.barrier()
    p.finalize()
    return nc


def kernel(**inputs):
    cons = host_consts()
    cons.update(host_ssm_consts())
    cons.update(host_l1f_consts())
    nc = build_full_program()
    in_maps = []
    for b in range(B):
        m = dict(cons)
        m["xa"] = np.ascontiguousarray(inputs["L0_attn_x"][b])
        m["l0_ls"] = np.asarray(inputs["L0_attn_ln_scale"]).reshape(1, D)
        m["l0_lb"] = np.asarray(inputs["L0_attn_ln_bias"]).reshape(1, D)
        m["l0_wq"] = inputs["L0_attn_wq"]
        m["l0_wk"] = inputs["L0_attn_wk"]
        m["l0_wv"] = inputs["L0_attn_wv"]
        m["l0f_g"] = inputs["L0_ffn_ln_gamma"]
        m["l0f_b"] = inputs["L0_ffn_ln_beta"]
        m["l0f_q1"] = inputs["L0_ffn_fc1_q"]
        m["l0f_s1"] = inputs["L0_ffn_fc1_scales"]
        m["l0f_b1"] = np.asarray(inputs["L0_ffn_fc1_bias"]).reshape(1, -1)
        m["l0f_q2"] = inputs["L0_ffn_fc2_q"]
        m["l0f_s2"] = inputs["L0_ffn_fc2_scales"]
        m["l0f_b2"] = inputs["L0_ffn_fc2_bias"]
        m["l1a_xpw"] = inputs["L1_attn_x_proj_w"]
        m["l1a_dtw"] = inputs["L1_attn_dt_proj_w"]
        m["l1a_dtb"] = inputs["L1_attn_dt_proj_b"]
        m["l1a_alog"] = inputs["L1_attn_A_log"]
        m["l1a_dpar"] = inputs["L1_attn_D_param"]
        m["l1a_lna"] = inputs["L1_attn_ln_a"]
        m["l1a_lnb"] = inputs["L1_attn_ln_b"]
        m["l1f_wqkv"] = inputs["L1_ffn_Wqkv"]
        m["l1f_qs"] = inputs["L1_ffn_q_scale"]
        m["l1f_ks"] = inputs["L1_ffn_k_scale"]
        m["l1f_wo"] = inputs["L1_ffn_Wo"]
        m["l1f_lng"] = inputs["L1_ffn_ln_g"]
        m["l1f_lnb"] = inputs["L1_ffn_ln_b"]
        m["l1f_w1"] = inputs["L1_ffn_W1"]
        m["l1f_b1"] = inputs["L1_ffn_b1"]
        m["l1f_w2"] = inputs["L1_ffn_W2"]
        m["l1f_b2"] = inputs["L1_ffn_b2"]
        in_maps.append(m)
    res = run_bass_kernel_spmd(nc, in_maps, core_ids=list(range(B)))
    return np.stack([r["y"] for r in res.results])
